# version 13
# baseline (speedup 1.0000x reference)
import numpy as np
import ml_dtypes
from contextlib import ExitStack

import concourse.bass as bass
import concourse.mybir as mybir
from concourse.bass_utils import run_bass_kernel_spmd

F32 = mybir.dt.float32
BF16 = mybir.dt.bfloat16
U32 = mybir.dt.uint32
I32 = mybir.dt.int32
AF = mybir.ActivationFunctionType
ALU = mybir.AluOpType
AX = mybir.AxisListType

T = 2048
TK = 8192
D = 1024
H = 16
HD = 64
DH = 2048
NE = 8
NT = T // 128
CAP = 2560.0
LN_EPS = 1e-5
QK_EPS = 1e-6
MOE_EPS = 1e-6

DEBUG = []
SEQUENTIAL = False
STRICT = {"dve"}
LAST_RESULTS = None

ENGINES = ("pe", "act", "dve", "pool", "sync")
STREAMS = ("pe", "act", "dve", "pool", "sync", "d_sync", "d_pool", "cc")


class Ev:
    __slots__ = ("stream", "idx", "snap")

    def __init__(self, stream, idx, snap):
        self.stream, self.idx, self.snap = stream, idx, snap


class Buf:
    __slots__ = ("name", "writes", "readers")

    def __init__(self, name=""):
        self.name, self.writes, self.readers = name, [], []


class Op:
    __slots__ = ("eng", "fn", "stream", "idx", "waits", "ev", "marked", "inc")


def _merge(dst, src):
    for k, v in src.items():
        if dst.get(k, -1) < v:
            dst[k] = v


class Sched:
    def __init__(self):
        self.ops = []
        self.n = {s: 0 for s in STREAMS}
        self.clock = {e: {} for e in ENGINES}
        self.pending = {e: [] for e in ENGINES}

    def barrier(self):
        last = {}
        for o in self.ops:
            last[o.stream] = o
        evs = [o.ev for o in last.values()]
        for e in ENGINES:
            self.pending[e] = list(evs)

    def op(self, eng, fn, reads=(), writes=(), writes_append=(), waits=(),
           dma=False, cc=False):
        stream = "cc" if cc else (("d_" + eng) if dma else eng)
        inc = 16 if dma else 1
        need = list(waits)
        if SEQUENTIAL and self.ops:
            need.append(self.ops[-1].ev)
        for b in reads:
            need += b.writes
        for b in writes:
            need += b.writes + b.readers
        for b in writes_append:
            need += b.readers
        need += self.pending[eng]
        self.pending[eng] = []
        clk = self.clock[eng]
        by_stream = {}
        for ev in need:
            if clk.get(ev.stream, -1) >= ev.idx:
                continue
            cur = by_stream.get(ev.stream)
            if cur is None or cur.idx < ev.idx:
                by_stream[ev.stream] = ev
        emit_waits = list(by_stream.values())
        for ev in emit_waits:
            _merge(clk, ev.snap)
        idx = self.n[stream]
        self.n[stream] += 1
        snap = dict(clk)
        if stream == eng:
            if eng not in STRICT:
                clk[eng] = idx
            snap[eng] = idx
        else:
            snap[eng] = self.n[eng] - 1
            snap[stream] = idx
        ev = Ev(stream, idx, snap)
        o = Op()
        o.eng, o.fn, o.stream, o.idx = eng, fn, stream, idx
        o.waits, o.ev, o.marked, o.inc = emit_waits, ev, False, inc
        self.ops.append(o)
        for b in reads:
            b.readers.append(ev)
        for b in writes:
            b.writes = [ev]
            b.readers = []
        for b in writes_append:
            b.writes.append(ev)
        return ev

    def emit(self, nc):
        lastop = {}
        for o in self.ops:
            lastop[(o.stream, o.idx)] = o
            if o.stream.startswith("d_") or o.stream == "cc":
                o.marked = True
        for o in self.ops:
            for ev in o.waits:
                lastop[(ev.stream, ev.idx)].marked = True
        last_stream = {}
        for o in self.ops:
            last_stream[o.stream] = o
        for o in last_stream.values():
            o.marked = True
        cum = {s: 0 for s in STREAMS}
        semval = {}
        for o in self.ops:
            if o.marked:
                cum[o.stream] += o.inc
                semval[(o.stream, o.idx)] = cum[o.stream]
        stack = ExitStack()
        sems = {s: stack.enter_context(nc.semaphore("sem_" + s))
                for s in STREAMS if cum[s] > 0}

        def section(engobj, ename):
            for o in self.ops:
                if o.eng != ename:
                    continue
                for ev in o.waits:
                    engobj.wait_ge(sems[ev.stream], semval[(ev.stream, ev.idx)])
                inst = o.fn(engobj)
                if o.marked:
                    inst.then_inc(sems[o.stream], o.inc)

        return section, stack


def build_program():
    nc = bass.Bass("TRN2", target_bir_lowering=False)
    s = Sched()
    op = s.op

    x_in = nc.dram_tensor("x_in", [T, D], F32, kind="ExternalInput").ap()
    wqkv = nc.dram_tensor("wqkv", [D, 3 * D], F32, kind="ExternalInput").ap()
    wo_d = nc.dram_tensor("wo", [D, D], F32, kind="ExternalInput").ap()
    bo_d = nc.dram_tensor("bo", [1, D], F32, kind="ExternalInput").ap()
    wr_d = nc.dram_tensor("wrout", [NE, D], F32, kind="ExternalInput").ap()
    wg_d = nc.dram_tensor("wgate", [DH, D], F32, kind="ExternalInput").ap()
    wd_d = nc.dram_tensor("wdown", [NE * D, DH], F32, kind="ExternalInput").ap()
    posv = nc.dram_tensor("posv", [128, 1], F32, kind="ExternalInput").ap()
    prefm = nc.dram_tensor("prefm", [NE, 1], F32, kind="ExternalInput").ap()
    y_out = nc.dram_tensor("y_out", [T, D], F32, kind="ExternalOutput").ap()

    dbg_outs = {}
    def dbg(name, shape, dt=F32):
        if name in DEBUG and name not in dbg_outs:
            dbg_outs[name] = nc.dram_tensor("dbg_" + name, shape, dt,
                                            kind="ExternalOutput").ap()
        return dbg_outs.get(name)

    def dump(name, ap, bufs, dt=F32):
        d = dbg(name, list(ap.shape), dt)
        if d is not None:
            s.op("sync", lambda e, d=d, ap=ap: e.dma_start(out=d, in_=ap),
                 dma=True, reads=list(bufs))

    cc_ki = nc.dram_tensor("cc_ki", [D, T], BF16, kind="Internal").ap()
    cc_vi = nc.dram_tensor("cc_vi", [T, D], BF16, kind="Internal").ap()
    cc_kg = nc.dram_tensor("cc_kg", [4 * D, T], BF16, kind="Internal").ap()
    cc_vg = nc.dram_tensor("cc_vg", [TK, D], BF16, kind="Internal").ap()
    cc_ci = nc.dram_tensor("cc_ci", [1, NE], F32, kind="Internal").ap()
    cc_cg = nc.dram_tensor("cc_cg", [NE, NE], F32, kind="Internal").ap()
    wdt_d = nc.dram_tensor("wdt_d", [NE * 16 * 128, 8 * 128], BF16,
                           kind="Internal").ap()
    wgt_d = nc.dram_tensor("wgt_d", [8 * 128, 16 * 128], BF16,
                           kind="Internal").ap()
    carr_d = nc.dram_tensor("carr_d", [NT, NE], F32, kind="Internal").ap()
    b_cck, b_ccv, b_ccko, b_ccvo = Buf(), Buf(), Buf(), Buf()
    b_ccc_i, b_ccc_o = Buf(), Buf()
    b_wdt_d, b_wgt_d, b_carr_d = Buf(), Buf(), Buf()

    ctx = ExitStack()
    SB = lambda nm, shp, dt: ctx.enter_context(nc.sbuf_tensor(nm, shp, dt))
    PS = lambda nm, shp, dt: ctx.enter_context(nc.psum_tensor(nm, shp, dt))

    ps_a = PS("ps_a", [128, 2048], F32)
    ps_o = PS("ps_o", [128, 512], F32)
    ps_c = PS("ps_c", [128, 512], F32)
    pst_ctx = ExitStack()
    ps_t = pst_ctx.enter_context(nc.psum_tensor("ps_t", [128, 1024], BF16))
    ps_t2 = pst_ctx.enter_context(nc.psum_tensor("ps_t2", [128, 1024], BF16))
    b_a0, b_a1, b_b0, b_b1 = Buf("a0"), Buf("a1"), Buf("a2"), Buf("a3")
    b_po, b_pc, b_pt, b_pt2 = Buf("po"), Buf("pc"), Buf("pt"), Buf("pt2")
    A0 = ps_a[:, 0:512]
    A1 = ps_a[:, 512:1024]
    B0 = ps_a[:, 1024:1536]
    B1 = ps_a[:, 1536:2048]
    BANKS = [(A0, b_a0), (A1, b_a1), (B0, b_b0), (B1, b_b1)]
    BANKBUFS = [b_a0, b_a1, b_b0, b_b1]

    ident = SB("ident", [128, 128], BF16)
    ident32 = SB("ident32", [128, 128], F32)
    onesr32 = SB("onesr32", [1, 128], F32)
    onescol = SB("onescol", [128, 1], F32)
    tri_incl = SB("tri_incl", [128, 128], F32)
    sl16 = SB("sl16", [16, 16], F32)
    iota8 = SB("iota8", [128, NE], F32)
    sel8 = SB("sel8", [NE, NE * 128], BF16)
    wrt = SB("wrt", [128, 8 * NE], BF16)
    wgt_sb = SB("wgt_sb", [128, 8 * DH], BF16)
    comb = SB("comb", [128, NT * NE], F32)
    b_ident, b_ident32, b_ones, b_onescol = Buf(), Buf(), Buf(), Buf()
    b_tri, b_sl16, b_iota8, b_sel8 = Buf(), Buf(), Buf(), Buf()
    b_wrt, b_wgt_sb = Buf(), Buf()
    b_comb = [Buf() for _ in range(NT)]

    scope_attn = ExitStack()
    attn_fm = scope_attn.enter_context(
        nc.sbuf_tensor("attn_fm", [128, 8 * T], BF16, side="right"))
    scope_q = ExitStack()
    q_fm = scope_q.enter_context(nc.sbuf_tensor("q_fm", [128, 8 * T], BF16))
    b_qfm = [Buf(f"qfm{i}") for i in range(8)]
    b_attn = [Buf(f"attn{i}") for i in range(8)]

    for t_, b_ in ((ident, b_ident), (ident32, b_ident32)):
        op("pool", lambda e, t_=t_: e.memset(t_[:], 0.0), writes=[b_])
        op("pool", lambda e, t_=t_: e.affine_select(
            out=t_[:], in_=t_[:], compare_op=ALU.not_equal, fill=1.0,
            base=0, pattern=[[-1, 128]], channel_multiplier=1),
            reads=[b_], writes=[b_])
    op("pool", lambda e: e.memset(onesr32[:], 1.0), writes=[b_ones])
    op("pool", lambda e: e.memset(onescol[:], 1.0), writes=[b_onescol])
    op("pool", lambda e: e.memset(tri_incl[:], 1.0), writes=[b_tri])
    op("pool", lambda e: e.affine_select(
        out=tri_incl[:], in_=tri_incl[:], compare_op=ALU.is_ge, fill=0.0,
        base=0, pattern=[[1, 128]], channel_multiplier=-1),
        reads=[b_tri], writes=[b_tri])
    op("pool", lambda e: e.memset(sl16[:], 1.0), writes=[b_sl16])
    op("pool", lambda e: e.affine_select(
        out=sl16[:], in_=sl16[:], compare_op=ALU.is_gt, fill=0.0,
        base=0, pattern=[[1, 16]], channel_multiplier=-1),
        reads=[b_sl16], writes=[b_sl16])
    op("pool", lambda e: e.iota(iota8[:], pattern=[[1, NE]], base=0,
                                channel_multiplier=0,
                                allow_small_or_imprecise_dtypes=True),
        writes=[b_iota8])
    op("pool", lambda e: e.memset(sel8[:], 0.0), writes=[b_sel8])
    for ei in range(NE):
        op("pool", lambda e, ei=ei: e.affine_select(
            out=sel8[:, ei * 128:(ei + 1) * 128],
            in_=sel8[:, ei * 128:(ei + 1) * 128],
            compare_op=ALU.not_equal, fill=1.0, base=-ei,
            pattern=[[0, 128]], channel_multiplier=1),
            reads=[b_sel8], writes_append=[b_sel8])

    rctx = ExitStack()
    RSB = lambda nm, shp, dt: rctx.enter_context(nc.sbuf_tensor(nm, shp, dt))
    ctab = RSB("ctab", [128, NT * 64], F32)
    stab = RSB("stab", [128, NT * 64], F32)
    b_ctab = [Buf() for _ in range(NT)]
    b_stab = [Buf() for _ in range(NT)]
    r_pos = RSB("r_pos", [128, 1], F32)
    r_invf = RSB("r_invf", [128, 32], F32)
    r_th = RSB("r_th", [128, 32], F32)
    r_u = RSB("r_u", [128, 32], F32)
    r_ni = RSB("r_ni", [128, 32], I32)
    r_n = RSB("r_n", [128, 32], F32)
    r_red = RSB("r_red", [128, 32], F32)
    r_sc = RSB("r_sc", [128, 32], F32)
    b_pos, b_invf, b_th, b_u = Buf(), Buf(), Buf(), Buf()
    b_ni, b_n, b_red, b_rsc = Buf(), Buf(), Buf(), Buf()

    op("sync", lambda e: e.dma_start(out=r_pos[:], in_=posv), dma=True,
       writes=[b_pos])
    op("pool", lambda e: e.iota(r_invf[:], pattern=[[1, 32]], base=0,
                                channel_multiplier=0,
                                allow_small_or_imprecise_dtypes=True),
        writes=[b_invf])
    op("act", lambda e: e.activation(r_invf[:], r_invf[:], AF.Exp,
                                     scale=float(-np.log(10000.0) / 32.0)),
        reads=[b_invf], writes=[b_invf])
    TWO_PI = float(2.0 * np.pi)
    C1 = 6.28125
    C2 = float(2.0 * np.pi - 6.28125)
    for tc in range(NT):
        op("pool", lambda e, tc=tc: e.iota(
            r_th[:], pattern=[[0, 32]], base=tc * 128, channel_multiplier=1,
            allow_small_or_imprecise_dtypes=True), writes=[b_th])
        op("dve", lambda e: e.tensor_scalar(r_th[:], r_th[:], r_pos[:], None,
                                            ALU.add),
            reads=[b_pos], writes=[b_th])
        op("dve", lambda e: e.tensor_mul(r_th[:], r_th[:], r_invf[:]),
            reads=[b_invf], writes=[b_th])
        for half in (0, 1):
            badd = 0.0 if half == 0 else 0.25
            op("dve", lambda e, badd=badd: e.tensor_scalar(
                r_u[:], r_th[:], 1.0 / TWO_PI, badd, ALU.mult, ALU.add),
                reads=[b_th], writes=[b_u])
            op("dve", lambda e: e.tensor_copy(r_ni[:], r_u[:]),
                reads=[b_u], writes=[b_ni])
            op("dve", lambda e: e.tensor_copy(r_n[:], r_ni[:]),
                reads=[b_ni], writes=[b_n])
            op("dve", lambda e: e.scalar_tensor_tensor(
                r_red[:], r_n[:], -C1, r_th[:], ALU.mult, ALU.add),
                reads=[b_n, b_th], writes=[b_red])
            sh = 0.0 if half == 0 else float(np.pi / 2.0)
            op("dve", lambda e: e.scalar_tensor_tensor(
                r_red[:], r_n[:], -C2, r_red[:], ALU.mult, ALU.add),
                reads=[b_n], writes=[b_red])
            if half == 1:
                op("dve", lambda e, sh=sh: e.tensor_scalar(
                    r_red[:], r_red[:], sh, None, ALU.add),
                    reads=[b_red], writes=[b_red])
            op("act", lambda e: e.activation(r_sc[:], r_red[:], AF.Sin),
                reads=[b_red], writes=[b_rsc])
            if half == 0:
                op("dve", lambda e, tc=tc: e.tensor_scalar(
                    stab[:, tc * 64:tc * 64 + 32], r_sc[:], -1.0, None,
                    ALU.mult), reads=[b_rsc], writes=[b_stab[tc]])
                op("dve", lambda e, tc=tc: e.tensor_copy(
                    stab[:, tc * 64 + 32:tc * 64 + 64], r_sc[:]),
                    reads=[b_rsc], writes_append=[b_stab[tc]])
            else:
                op("dve", lambda e, tc=tc: e.tensor_copy(
                    ctab[:, tc * 64:tc * 64 + 32], r_sc[:]),
                    reads=[b_rsc], writes=[b_ctab[tc]])
                op("dve", lambda e, tc=tc: e.tensor_copy(
                    ctab[:, tc * 64 + 32:tc * 64 + 64], r_sc[:]),
                    reads=[b_rsc], writes_append=[b_ctab[tc]])

    dump("ctab0", ctab[:], b_ctab)
    dump("stab0", stab[:], b_stab)
    dump("invf", r_invf[:], [b_invf])

    wctx = ExitStack()
    WSB = lambda nm, shp, dt: wctx.enter_context(nc.sbuf_tensor(nm, shp, dt))
    wd_in = WSB("wd_in", [128, 2 * DH], BF16)
    wd_st = WSB("wd_st", [128, 2 * DH], BF16)
    wr_raw = WSB("wr_raw", [NE, D], F32)
    b_wd_in = [Buf(), Buf()]
    b_wd_st = [Buf(), Buf()]
    b_wr_raw = Buf()

    op("sync", lambda e: e.dma_start(out=wr_raw[:], in_=wr_d), dma=True,
       writes=[b_wr_raw])
    for dc in range(8):
        op("pe", lambda e, dc=dc: e.transpose(
            ps_c[:, 0:NE], wr_raw[:, dc * 128:(dc + 1) * 128],
            ident32[0:NE, 0:NE]),
            reads=[b_wr_raw, b_ident32], writes=[b_pc])
        op("dve", lambda e, dc=dc: e.tensor_copy(
            wrt[:, dc * NE:(dc + 1) * NE], ps_c[:, 0:NE]),
            reads=[b_pc], writes_append=[b_wrt])

    wdt4 = wdt_d.rearrange("(ei hc p) z -> ei hc p z", ei=NE, hc=16)
    for ei in range(NE):
        for dc in range(8):
            k = ei * 8 + dc
            sl = k % 2
            pst, b_pst = ((ps_t, b_pt) if sl == 0 else (ps_t2, b_pt2))
            op("pool", lambda e, k=k, sl=sl: e.dma_start(
                out=wd_in[:, sl * DH:(sl + 1) * DH],
                in_=wd_d[k * 128:(k + 1) * 128, :]),
                dma=True, writes=[b_wd_in[sl]])
            for hg in range(2):
                for hh in range(8):
                    hc = hg * 8 + hh
                    op("pe", lambda e, sl=sl, hc=hc, hh=hh, pst=pst: e.transpose(
                        pst[:, hh * 128:(hh + 1) * 128],
                        wd_in[:, sl * DH + hc * 128:sl * DH + (hc + 1) * 128],
                        ident[:]),
                        reads=[b_wd_in[sl], b_ident],
                        writes=[b_pst] if hh == 0 else (),
                        writes_append=[b_pst] if hh > 0 else ())
                ev_eng = "dve" if pst is ps_t else "act"
                for hh in range(8):
                    hc = hg * 8 + hh
                    if ev_eng == "dve":
                        op("dve", lambda e, sl=sl, hc=hc, hh=hh, pst=pst:
                            e.tensor_copy(
                                wd_st[:, sl * DH + hc * 128:
                                      sl * DH + (hc + 1) * 128],
                                pst[:, hh * 128:(hh + 1) * 128]),
                            reads=[b_pst], writes_append=[b_wd_st[sl]])
                    else:
                        op("act", lambda e, sl=sl, hc=hc, hh=hh, pst=pst:
                            e.copy(
                                wd_st[:, sl * DH + hc * 128:
                                      sl * DH + (hc + 1) * 128],
                                pst[:, hh * 128:(hh + 1) * 128]),
                            reads=[b_pst], writes_append=[b_wd_st[sl]])
            for hg in range(2):
                op("sync", lambda e, ei=ei, dc=dc, sl=sl, hg=hg: e.dma_start(
                    out=wdt4[ei, hg * 8:(hg + 1) * 8, :, dc * 128:(dc + 1) * 128]
                    .transpose([1, 0, 2]),
                    in_=wd_st[:, sl * DH + hg * 1024:sl * DH + (hg + 1) * 1024]
                    .rearrange("p (hc z) -> p hc z", z=128)),
                    dma=True, reads=[b_wd_st[sl]], writes_append=[b_wdt_d])

    wgt4 = wgt_d.rearrange("(dc p) z -> dc p z", dc=8)
    for hc in range(16):
        sl = hc % 2
        pst, b_pst = ((ps_t, b_pt) if sl == 0 else (ps_t2, b_pt2))
        op("pool", lambda e, hc=hc, sl=sl: e.dma_start(
            out=wd_in[:, sl * DH:sl * DH + D],
            in_=wg_d[hc * 128:(hc + 1) * 128, :]),
            dma=True, writes=[b_wd_in[sl]])
        for dc in range(8):
            op("pe", lambda e, sl=sl, dc=dc, pst=pst: e.transpose(
                pst[:, dc * 128:(dc + 1) * 128],
                wd_in[:, sl * DH + dc * 128:sl * DH + (dc + 1) * 128],
                ident[:]),
                reads=[b_wd_in[sl], b_ident],
                writes=[b_pst] if dc == 0 else (),
                writes_append=[b_pst] if dc > 0 else ())
        eng = "dve" if pst is ps_t else "act"
        for dc in range(8):
            fn = (lambda e, sl=sl, dc=dc, pst=pst: e.tensor_copy(
                wd_st[:, sl * DH + dc * 128:sl * DH + (dc + 1) * 128],
                pst[:, dc * 128:(dc + 1) * 128])) if eng == "dve" else \
                (lambda e, sl=sl, dc=dc, pst=pst: e.copy(
                    wd_st[:, sl * DH + dc * 128:sl * DH + (dc + 1) * 128],
                    pst[:, dc * 128:(dc + 1) * 128]))
            op(eng, fn, reads=[b_pst], writes_append=[b_wd_st[sl]])
        op("sync", lambda e, hc=hc, sl=sl: e.dma_start(
            out=wgt4[:, :, hc * 128:(hc + 1) * 128].transpose([1, 0, 2]),
            in_=wd_st[:, sl * DH:sl * DH + D].rearrange(
                "p (dc z) -> p dc z", z=128)),
            dma=True, reads=[b_wd_st[sl]], writes_append=[b_wgt_d])
    wctx.close()

    actx = ExitStack()
    ASB = lambda nm, shp, dt: actx.enter_context(nc.sbuf_tensor(nm, shp, dt))
    xt = ASB("xt", [128, 2 * D], F32)
    xn1 = ASB("xn1", [128, 8 * T], BF16)
    xn_tm = ASB("xn_tm", [128, 2 * D], BF16)
    wq_blk = ASB("wq_blk", [128, 2 * 8 * 512], BF16)
    qkv_blk = ASB("qkv_blk", [128, NT * 512], BF16)
    kstage = ASB("kstage", [128, 2048], BF16)
    lns = ASB("lns", [128, 16], F32)
    lnsq = ASB("lnsq", [128, 2 * D], F32)
    rp = ASB("rp", [128, 2 * 4 * 256], BF16)
    qnsq = ASB("qnsq", [128, 2 * 512], F32)
    qnss = ASB("qnss", [128, 16], F32)
    b_xt = [Buf(), Buf()]
    b_xn1 = [Buf(f"xn1_{i}") for i in range(8)]
    b_xn_tm = [Buf(), Buf()]
    b_wq = [Buf(), Buf()]
    b_qkvb = [Buf(f"qkvb{t}") for t in range(NT)]
    b_kstage = Buf()
    b_lns = [Buf(), Buf()]
    b_lnsq = [Buf(), Buf()]
    b_rp = [Buf(), Buf()]
    b_qnsq = [Buf(), Buf()]
    b_qnss = [Buf(), Buf()]

    def emit_ln(src_ap, b_src_list, dst_ap, b_dst, par=0):
        o = par * 8
        b_lnsP = b_lns[par]
        b_lnsqP = b_lnsq[par]
        lnsqv = lnsq[:, par * D:(par + 1) * D]
        sl_s1 = lns[:, o + 0:o + 1]
        sl_s2 = lns[:, o + 1:o + 2]
        sl_m = lns[:, o + 2:o + 3]
        sl_v = lns[:, o + 3:o + 4]
        sl_r = lns[:, o + 4:o + 5]
        sl_nm = lns[:, o + 5:o + 6]
        op("act", lambda e: e.activation(lnsqv, src_ap, AF.Identity,
                                         accum_out=sl_s1),
            reads=b_src_list, writes=[b_lnsqP], writes_append=[b_lnsP])
        op("act", lambda e: e.activation(lnsqv, src_ap, AF.Square,
                                         accum_out=sl_s2),
            reads=b_src_list, writes=[b_lnsqP], writes_append=[b_lnsP])
        op("dve", lambda e: e.tensor_scalar(sl_m, sl_s1, 1.0 / D, None,
                                            ALU.mult),
            reads=[b_lnsP], writes_append=[b_lnsP])
        op("dve", lambda e: e.tensor_scalar(sl_v, sl_s2, 1.0 / D, None,
                                            ALU.mult),
            reads=[b_lnsP], writes_append=[b_lnsP])
        op("dve", lambda e: e.scalar_tensor_tensor(
            sl_nm, sl_m, -1.0, sl_m, ALU.mult, ALU.mult),
            reads=[b_lnsP], writes_append=[b_lnsP])
        op("dve", lambda e: e.tensor_add(sl_v, sl_v, sl_nm),
            reads=[b_lnsP], writes_append=[b_lnsP])
        op("dve", lambda e: e.tensor_scalar(sl_v, sl_v, LN_EPS, None,
                                            ALU.add),
            reads=[b_lnsP], writes_append=[b_lnsP])
        op("act", lambda e: e.activation(sl_v, sl_v, AF.Ln),
            reads=[b_lnsP], writes_append=[b_lnsP])
        op("act", lambda e: e.activation(sl_r, sl_v, AF.Exp, scale=-0.5),
            reads=[b_lnsP], writes_append=[b_lnsP])
        op("dve", lambda e: e.tensor_scalar(sl_nm, sl_m, sl_r, -1.0,
                                            ALU.mult, ALU.mult),
            reads=[b_lnsP], writes_append=[b_lnsP])
        op("dve", lambda e: e.tensor_scalar(dst_ap, src_ap, sl_r, sl_nm,
                                            ALU.mult, ALU.add),
            reads=b_src_list + [b_lnsP], writes=[b_dst])

    for tc in range(NT):
        sl = tc % 2
        op("sync", lambda e, tc=tc, sl=sl: e.dma_start(
            out=xt[:, sl * D:(sl + 1) * D],
            in_=x_in[tc * 128:(tc + 1) * 128, :]),
            dma=True, writes=[b_xt[sl]])
        emit_ln(xt[:, sl * D:(sl + 1) * D], [b_xt[sl]],
                xn_tm[:, sl * D:(sl + 1) * D], b_xn_tm[sl], par=sl)
        pst, b_pst = ((ps_t, b_pt) if sl == 0 else (ps_t2, b_pt2))
        for dc in range(8):
            op("pe", lambda e, sl=sl, dc=dc, pst=pst: e.transpose(
                pst[:, dc * 128:(dc + 1) * 128],
                xn_tm[:, sl * D + dc * 128:sl * D + (dc + 1) * 128],
                ident[:]),
                reads=[b_xn_tm[sl], b_ident],
                writes=[b_pst] if dc == 0 else (),
                writes_append=[b_pst] if dc > 0 else ())
        eng = "dve" if pst is ps_t else "act"
        for dc in range(8):
            fn = (lambda e, tc=tc, dc=dc, pst=pst: e.tensor_copy(
                xn1[:, dc * T + tc * 128:dc * T + (tc + 1) * 128],
                pst[:, dc * 128:(dc + 1) * 128])) if eng == "dve" else \
                (lambda e, tc=tc, dc=dc, pst=pst: e.copy(
                    xn1[:, dc * T + tc * 128:dc * T + (tc + 1) * 128],
                    pst[:, dc * 128:(dc + 1) * 128]))
            op(eng, fn, reads=[b_pst], writes_append=[b_xn1[dc]])

    for blk in range(6):
        sl = blk % 2
        for dc in range(8):
            op("pool", lambda e, blk=blk, sl=sl, dc=dc: e.dma_start(
                out=wq_blk[:, sl * 4096 + dc * 512:sl * 4096 + (dc + 1) * 512],
                in_=wqkv[dc * 128:(dc + 1) * 128, blk * 512:(blk + 1) * 512]),
                dma=True, writes=[b_wq[sl]] if dc == 0 else (),
                writes_append=[b_wq[sl]] if dc > 0 else ())
        for tc in range(NT):
            pbank, b_pb = BANKS[tc % 4]
            for dc in range(8):
                op("pe", lambda e, blk=blk, sl=sl, tc=tc, dc=dc, pbank=pbank:
                    e.matmul(
                        pbank,
                        xn1[:, dc * T + tc * 128:dc * T + (tc + 1) * 128],
                        wq_blk[:, sl * 4096 + dc * 512:
                               sl * 4096 + (dc + 1) * 512],
                        start=(dc == 0), stop=(dc == 7)),
                    reads=[b_xn1[dc], b_wq[sl]],
                    writes=[b_pb] if dc == 0 else (),
                    writes_append=[b_pb] if dc > 0 else ())
            op("dve", lambda e, tc=tc, pbank=pbank: e.tensor_copy(
                qkv_blk[:, tc * 512:(tc + 1) * 512], pbank),
                reads=[b_pb], writes=[b_qkvb[tc]])

        if blk < 4:
            for tc in range(NT):
                base = tc * 512
                q4 = qkv_blk[:, base:base + 512].rearrange(
                    "p (h b j) -> p h b j", b=2, j=32)
                x1 = q4[:, :, 0, :]
                x2 = q4[:, :, 1, :]
                c32 = ctab[:, tc * 64:tc * 64 + 32].unsqueeze(1) \
                    .broadcast_to([128, 8, 32])
                s32n = stab[:, tc * 64:tc * 64 + 32].unsqueeze(1) \
                    .broadcast_to([128, 8, 32])
                s32p = stab[:, tc * 64 + 32:tc * 64 + 64].unsqueeze(1) \
                    .broadcast_to([128, 8, 32])
                par = tc % 2
                b_rpP = b_rp[par]
                r3 = [rp[:, par * 1024 + i * 256:par * 1024 + (i + 1) * 256]
                      .rearrange("p (h j) -> p h j", j=32) for i in range(4)]
                op("pool", lambda e, x1=x1, c32=c32, r3=r3: e.tensor_tensor(
                    r3[0], x1, c32, ALU.mult),
                    reads=[b_qkvb[tc], b_ctab[tc]], writes=[b_rpP])
                op("pool", lambda e, x2=x2, s32n=s32n, r3=r3: e.tensor_tensor(
                    r3[1], x2, s32n, ALU.mult),
                    reads=[b_qkvb[tc], b_stab[tc]], writes_append=[b_rpP])
                op("pool", lambda e, x1=x1, s32p=s32p, r3=r3: e.tensor_tensor(
                    r3[2], x1, s32p, ALU.mult),
                    reads=[b_qkvb[tc], b_stab[tc]], writes_append=[b_rpP])
                op("pool", lambda e, x2=x2, c32=c32, r3=r3: e.tensor_tensor(
                    r3[3], x2, c32, ALU.mult),
                    reads=[b_qkvb[tc], b_ctab[tc]], writes_append=[b_rpP])
                op("dve", lambda e, x1=x1, r3=r3: e.tensor_tensor(
                    x1, r3[0], r3[1], ALU.add),
                    reads=[b_rpP], writes_append=[b_qkvb[tc]])
                op("dve", lambda e, x2=x2, r3=r3: e.tensor_tensor(
                    x2, r3[2], r3[3], ALU.add),
                    reads=[b_rpP], writes_append=[b_qkvb[tc]])
                qb2 = qkv_blk[:, base:base + 512]
                qb3 = qb2.rearrange("p (h j) -> p h j", j=64)
                qnsqv = qnsq[:, par * 512:(par + 1) * 512]
                qnssv = qnss[:, par * 8:(par + 1) * 8]
                b_qnsqP, b_qnssP = b_qnsq[par], b_qnss[par]
                op("pool", lambda e, qb2=qb2, qnsqv=qnsqv: e.tensor_tensor(
                    qnsqv, qb2, qb2, ALU.mult),
                    reads=[b_qkvb[tc]], writes=[b_qnsqP])
                op("dve", lambda e, qnsqv=qnsqv, qnssv=qnssv: e.reduce_sum(
                    qnssv, qnsqv.rearrange("p (h j) -> p h j", j=64),
                    axis=AX.X),
                    reads=[b_qnsqP], writes=[b_qnssP])
                op("dve", lambda e, qnssv=qnssv: e.tensor_scalar(
                    qnssv, qnssv, QK_EPS, None, ALU.add),
                    reads=[b_qnssP], writes=[b_qnssP])
                op("act", lambda e, qnssv=qnssv: e.activation(
                    qnssv, qnssv, AF.Ln),
                    reads=[b_qnssP], writes=[b_qnssP])
                op("act", lambda e, qnssv=qnssv: e.activation(
                    qnssv, qnssv, AF.Exp, scale=-0.5),
                    reads=[b_qnssP], writes=[b_qnssP])
                rb = qnssv.unsqueeze(2).broadcast_to([128, 8, 64])
                op("dve", lambda e, qb3=qb3, rb=rb: e.tensor_tensor(
                    qb3, qb3, rb, ALU.mult),
                    reads=[b_qnssP], writes_append=[b_qkvb[tc]])

            for b2 in range(4):
                ch = blk * 4 + b2
                pst, b_pst = ((ps_t, b_pt) if b2 % 2 == 0 else (ps_t2, b_pt2))
                for tg in range(2):
                    for th in range(8):
                        tc = tg * 8 + th
                        op("pe", lambda e, tc=tc, th=th, b2=b2, pst=pst:
                            e.transpose(
                                pst[:, th * 128:(th + 1) * 128],
                                qkv_blk[:, tc * 512 + b2 * 128:
                                        tc * 512 + (b2 + 1) * 128],
                                ident[:]),
                            reads=[b_qkvb[tc], b_ident],
                            writes=[b_pst] if th == 0 else (),
                            writes_append=[b_pst] if th > 0 else ())
                    for th in range(8):
                        tc = tg * 8 + th
                        if blk < 2:
                            dst = q_fm[:, (ch % 8) * T + tc * 128:
                                       (ch % 8) * T + (tc + 1) * 128]
                            bd = b_qfm[ch % 8]
                        else:
                            dst = kstage[:, tc * 128:(tc + 1) * 128]
                            bd = b_kstage
                        if pst is ps_t:
                            op("dve", lambda e, dst=dst, th=th, pst=pst:
                                e.tensor_copy(dst,
                                              pst[:, th * 128:(th + 1) * 128]),
                                reads=[b_pst], writes_append=[bd])
                        else:
                            op("act", lambda e, dst=dst, th=th, pst=pst:
                                e.copy(dst, pst[:, th * 128:(th + 1) * 128]),
                                reads=[b_pst], writes_append=[bd])
                if blk >= 2:
                    r0 = (blk - 2) * 512 + b2 * 128
                    op("sync", lambda e, r0=r0: e.dma_start(
                        out=cc_ki[r0:r0 + 128, :], in_=kstage[:]),
                        dma=True, reads=[b_kstage], writes_append=[b_cck])
                    b_kstage.writes = []
        else:
            for tc in range(NT):
                op("sync", lambda e, blk=blk, tc=tc: e.dma_start(
                    out=cc_vi[tc * 128:(tc + 1) * 128,
                                (blk - 4) * 512:(blk - 3) * 512],
                    in_=qkv_blk[:, tc * 512:(tc + 1) * 512]),
                    dma=True, reads=[b_qkvb[tc]], writes_append=[b_ccv])

    dump("xn1", xn1[:], b_xn1, BF16)
    dump("qfm", q_fm[:], b_qfm, BF16)
    dump("ctab", ctab[:], b_ctab)
    dump("stab", stab[:], b_stab)
    dump("kci", cc_ki, [b_cck], BF16)
    dump("vci", cc_vi, [b_ccv], BF16)

    ev_k = op("pool", lambda e: e.collective_compute(
        "AllGather", ALU.bypass,
        replica_groups=[[0, 1, 2, 3], [4, 5, 6, 7]],
        ins=[cc_ki.opt()], outs=[cc_kg.opt()]),
        cc=True, reads=[b_cck], writes=[b_ccko])
    ev_v = op("pool", lambda e: e.collective_compute(
        "AllGather", ALU.bypass,
        replica_groups=[[0, 1, 2, 3], [4, 5, 6, 7]],
        ins=[cc_vi.opt()], outs=[cc_vg.opt()]),
        cc=True, reads=[b_ccv], writes=[b_ccvo])

    for dc in range(8):
        op("sync", lambda e, dc=dc: e.dma_start(
            out=wgt_sb[:, dc * DH:(dc + 1) * DH],
            in_=wgt4[dc]),
            dma=True, waits=[],
            reads=[b_wgt_d],
            writes=[b_wgt_sb] if dc == 0 else (),
            writes_append=[b_wgt_sb] if dc > 0 else ())

    actx.close()
    rctx.close()
    pst_ctx.close()
    s.barrier()

    bctx = ExitStack()
    ps_ob = bctx.enter_context(nc.psum_tensor("ps_ob", [128, 512], F32))
    ps_cb = bctx.enter_context(nc.psum_tensor("ps_cb", [128, 512], F32))
    b_pob, b_pcb = Buf("pob"), Buf("pcb")
    BSB = lambda nm, shp, dt: bctx.enter_context(nc.sbuf_tensor(nm, shp, dt))
    kh = BSB("kh", [128, 2 * TK], BF16)
    vga = BSB("vga", [128, 2 * 2 * 4160], BF16)
    ptl = BSB("ptl", [128, 4 * 1024], BF16)
    zr = BSB("zr", [1, 2 * 512], F32)
    zb = BSB("zb", [64, 2 * 512], F32)
    b_kh = [Buf(), Buf()]
    b_vga = [Buf(), Buf()]
    b_ptl = [Buf(), Buf(), Buf(), Buf()]
    b_zr, b_zb = Buf(), Buf()

    for sl in range(2):
        op("pool", lambda e, sl=sl: e.memset(
            vga[:, sl * 8320:(sl + 1) * 8320], 1.0), writes=[b_vga[sl]])

    for hp in range(8):
        sl = hp % 2
        for v in range(2):
            h = 2 * hp + v
            for r in range(4):
                op("sync", lambda e, h=h, sl=sl, r=r, v=v: e.dma_start(
                    out=kh[v * 64:(v + 1) * 64,
                           sl * TK + r * T:sl * TK + (r + 1) * T],
                    in_=cc_kg[r * D + h * 64:r * D + h * 64 + 64, :]),
                    dma=True, waits=[ev_k],
                    writes=[b_kh[sl]] if (r == 0 and v == 0) else (),
                    writes_append=[b_kh[sl]] if (r > 0 or v > 0) else ())
        for v in range(2):
            h = 2 * hp + v
            for g in range(16):
                eng = "sync" if g % 2 == 0 else "pool"
                op(eng, lambda e, h=h, sl=sl, v=v, g=g: e.dma_start(
                    out=vga[:, sl * 8320 + v * 4160 + g * 4 * 65:
                            sl * 8320 + v * 4160 + (g + 1) * 4 * 65]
                    .rearrange("p (kt z) -> p kt z", z=65)[:, :, 0:64],
                    in_=cc_vg[g * 512:(g + 1) * 512, h * 64:(h + 1) * 64]
                    .rearrange("(kt p) z -> p kt z", p=128)),
                    dma=True, waits=[ev_v], writes_append=[b_vga[sl]])
        ch = hp
        for qb in range(4):
            qcol = ch * T + qb * 512
            if qb % 2 == 0:
                OB = [(ps_o, b_po), (ps_c, b_pc)]
            else:
                OB = [(ps_ob, b_pob), (ps_cb, b_pcb)]
            b_sh = [b_a0, b_a1]

            def emit_s(kt, sl=sl, qcol=qcol, ch=ch):
                psl = kt % 2
                for v in range(2):
                    op("pe", lambda e, sl=sl, kt=kt, v=v, psl=psl,
                        qcol=qcol: e.matmul(
                            ps_a[:, psl * 1024 + v * 512:
                                 psl * 1024 + (v + 1) * 512],
                            kh[v * 64:(v + 1) * 64,
                               sl * TK + kt * 128:sl * TK + (kt + 1) * 128],
                            q_fm[v * 64:(v + 1) * 64, qcol:qcol + 512],
                            start=True, stop=True),
                        reads=[b_kh[sl], b_qfm[ch]],
                        writes=[b_sh[psl]] if v == 0 else (),
                        writes_append=[b_sh[psl]] if v > 0 else ())

            emit_s(0)
            for kt in range(64):
                psl = kt % 2
                tsl4 = kt % 4
                op("act", lambda e, psl=psl, tsl4=tsl4: e.activation(
                    ptl[:, tsl4 * 1024:(tsl4 + 1) * 1024],
                    ps_a[:, psl * 1024:(psl + 1) * 1024],
                    AF.Exp, scale=0.125),
                    reads=[b_sh[psl]], writes=[b_ptl[tsl4]])
                if kt + 1 < 64:
                    emit_s(kt + 1)
                for v in range(2):
                    pso, b_pso = OB[v]
                    op("pe", lambda e, sl=sl, tsl4=tsl4, kt=kt, v=v,
                        pso=pso: e.matmul(
                            pso[0:65, :],
                            vga[:, sl * 8320 + v * 4160 + kt * 65:
                                sl * 8320 + v * 4160 + (kt + 1) * 65],
                            ptl[:, tsl4 * 1024 + v * 512:
                                tsl4 * 1024 + (v + 1) * 512],
                            start=(kt == 0), stop=(kt == 63)),
                        reads=[b_vga[sl], b_ptl[tsl4]],
                        writes=[b_pso] if kt == 0 else (),
                        writes_append=[b_pso] if kt > 0 else ())
            for v in range(2):
                pso, b_pso = OB[v]
                op("dve", lambda e, v=v, pso=pso: e.reciprocal(
                    zr[:, v * 512:(v + 1) * 512], pso[64:65, :]),
                    reads=[b_pso], writes=[b_zr] if v == 0 else (),
                    writes_append=[b_zr] if v == 1 else ())
            for v in range(2):
                pso, b_pso = OB[v]
                op("pe", lambda e, v=v: e.matmul(
                    ps_a[0:64, v * 1024:v * 1024 + 512], onesr32[:, 0:64],
                    zr[:, v * 512:(v + 1) * 512],
                    start=True, stop=True),
                    reads=[b_zr, b_ones], writes=[b_sh[v]])
                op("dve", lambda e, v=v: e.tensor_copy(
                    zb[:, v * 512:(v + 1) * 512],
                    ps_a[0:64, v * 1024:v * 1024 + 512]),
                    reads=[b_sh[v]],
                    writes=[b_zb] if v == 0 else (),
                    writes_append=[b_zb] if v == 1 else ())
                op("dve", lambda e, v=v, qcol=qcol, ch=ch, pso=pso:
                    e.tensor_tensor(
                        attn_fm[v * 64:(v + 1) * 64, qcol:qcol + 512],
                        pso[0:64, :], zb[:, v * 512:(v + 1) * 512], ALU.mult),
                    reads=[b_pso, b_zb], writes_append=[b_attn[ch]])

    dump("attn", attn_fm[:], b_attn, BF16)
    dump("kg", cc_kg, [b_ccko], BF16)
    dump("vg", cc_vg, [b_ccvo], BF16)
    bctx.close()
    scope_q.close()
    s.barrier()

    cdctx = ExitStack()
    CSB = lambda nm, shp, dt: cdctx.enter_context(nc.sbuf_tensor(nm, shp, dt))
    x_post = CSB("x_post", [128, NT * D], F32)
    b_xp = [Buf(f"xp{t}") for t in range(NT)]

    c1ctx = ExitStack()
    C1SB = lambda nm, shp, dt: c1ctx.enter_context(nc.sbuf_tensor(nm, shp, dt))
    wo_sb = C1SB("wo_sb", [128, 8 * D], BF16)
    bo_s = C1SB("bo_s", [1, D], F32)
    bo_bc = C1SB("bo_bc", [128, D], F32)
    xoc = C1SB("xoc", [128, 2 * D], F32)
    c1s = C1SB("c1s", [128, 512], F32)
    b_wo, b_bos, b_bobc, b_c1s = Buf(), Buf(), Buf(), Buf()
    b_xoc = [Buf(), Buf()]

    for dc in range(8):
        op("pool", lambda e, dc=dc: e.dma_start(
            out=wo_sb[:, dc * D:(dc + 1) * D],
            in_=wo_d[dc * 128:(dc + 1) * 128, :]),
            dma=True, writes=[b_wo] if dc == 0 else (),
            writes_append=[b_wo] if dc > 0 else ())
    op("sync", lambda e: e.dma_start(out=bo_s[:], in_=bo_d), dma=True,
       writes=[b_bos])
    for j in range(2):
        op("pe", lambda e, j=j: e.matmul(
            ps_c[:, 0:512], onesr32[:, 0:128], bo_s[:, j * 512:(j + 1) * 512],
            start=True, stop=True),
            reads=[b_bos, b_ones], writes=[b_pc])
        op("dve", lambda e, j=j: e.tensor_copy(
            bo_bc[:, j * 512:(j + 1) * 512], ps_c[:, 0:512]),
            reads=[b_pc],
            writes=[b_bobc] if j == 0 else (),
            writes_append=[b_bobc] if j > 0 else ())

    for tc in range(NT):
        sl = tc % 2
        op("sync", lambda e, tc=tc, sl=sl: e.dma_start(
            out=xoc[:, sl * D:(sl + 1) * D],
            in_=x_in[tc * 128:(tc + 1) * 128, :]),
            dma=True, writes=[b_xoc[sl]])
        for dh2 in range(2):
            pbank, b_pb = BANKS[(tc * 2 + dh2) % 4]
            for dc in range(8):
                op("pe", lambda e, tc=tc, dh2=dh2, dc=dc, pbank=pbank:
                    e.matmul(
                        pbank,
                        attn_fm[:, dc * T + tc * 128:dc * T + (tc + 1) * 128],
                        wo_sb[:, dc * D + dh2 * 512:dc * D + (dh2 + 1) * 512],
                        start=(dc == 0), stop=(dc == 7)),
                    reads=[b_attn[dc], b_wo],
                    writes=[b_pb] if dc == 0 else (),
                    writes_append=[b_pb] if dc > 0 else ())
            op("dve", lambda e, dh2=dh2, pbank=pbank: e.tensor_tensor(
                c1s[:], pbank, bo_bc[:, dh2 * 512:(dh2 + 1) * 512], ALU.add),
                reads=[b_pb, b_bobc], writes=[b_c1s])
            op("dve", lambda e, tc=tc, sl=sl, dh2=dh2: e.tensor_tensor(
                x_post[:, tc * D + dh2 * 512:tc * D + (dh2 + 1) * 512],
                c1s[:], xoc[:, sl * D + dh2 * 512:sl * D + (dh2 + 1) * 512],
                ALU.add),
                reads=[b_c1s, b_xoc[sl]], writes_append=[b_xp[tc]])

    dump("xpost", x_post[:], b_xp)
    c1ctx.close()
    scope_attn.close()
    s.barrier()
    pst_ctx = ExitStack()
    ps_t = pst_ctx.enter_context(nc.psum_tensor("ps_t3", [128, 1024], BF16))
    ps_t2 = pst_ctx.enter_context(nc.psum_tensor("ps_t4", [128, 1024], BF16))

    xn2 = CSB("xn2", [128, 8 * T], BF16)
    xn2_tm = CSB("xn2_tm", [128, D], BF16)
    lns2 = CSB("lns2", [128, 16], F32)
    lnsq2 = CSB("lnsq2", [128, D], F32)
    gts = CSB("gts", [128, NT * NE], F32)
    msk = CSB("msk", [128, NT * NE], F32)
    i12 = CSB("i12", [128, NT * 2], F32)
    g12 = CSB("g12", [128, NT * 2], F32)
    t8 = CSB("t8", [128, 8], F32)
    i8u = CSB("i8u", [128, 8], U32)
    r8a = CSB("r8a", [128, 8], F32)
    r8b = CSB("r8b", [128, 8], F32)
    r1a = CSB("r1a", [128, 8], F32)
    cs_s = CSB("cs_s", [NE, NT], F32)
    cnt_s = CSB("cnt_s", [NE, 1], F32)
    cnt_g = CSB("cnt_g", [NE, NE], F32)
    pm_s = CSB("pm_s", [NE, 1], F32)
    offs_s = CSB("offs_s", [NE, 1], F32)
    offsr = CSB("offsr", [1, NE], F32)
    cst = CSB("cst", [16, NE], F32)
    carrf = CSB("carrf", [1, 128], F32)
    carr_st = CSB("carr_st", [16, NE], F32)
    b_xn2 = [Buf(f"xn2_{i}") for i in range(8)]
    _b_x2t = Buf()
    b_xn2tm = [_b_x2t, _b_x2t]
    _b_lq2 = Buf()
    b_lns2 = [Buf(), Buf()]
    b_lnsq2 = [_b_lq2, _b_lq2]
    b_gts = [Buf() for _ in range(NT)]
    b_msk = [Buf() for _ in range(NT)]
    b_i12 = [Buf() for _ in range(NT)]
    b_g12 = [Buf() for _ in range(NT)]
    b_t8, b_i8u = Buf(), Buf()
    b_r8a, b_r8b, b_r1a = Buf(), Buf(), Buf()
    b_css, b_cnts, b_cntg, b_pms = Buf(), Buf(), Buf(), Buf()
    b_offs, b_offsr, b_cst, b_carrf, b_carrst = Buf(), Buf(), Buf(), Buf(), Buf()

    def emit_ln2(src_ap, b_src_list, dst_ap, b_dst, par=0):
        o = par * 8
        b_ln2P = b_lns2[par]
        b_lq2P = b_lnsq2[par]
        lnsq2v = lnsq2[:, 0:D]
        sl_s1 = lns2[:, o:o + 1]; sl_s2 = lns2[:, o + 1:o + 2]
        sl_m = lns2[:, o + 2:o + 3]
        sl_v = lns2[:, o + 3:o + 4]; sl_r = lns2[:, o + 4:o + 5]
        sl_nm = lns2[:, o + 5:o + 6]
        op("act", lambda e: e.activation(lnsq2v, src_ap, AF.Identity,
                                         accum_out=sl_s1),
            reads=b_src_list, writes=[b_lq2P], writes_append=[b_ln2P])
        op("act", lambda e: e.activation(lnsq2v, src_ap, AF.Square,
                                         accum_out=sl_s2),
            reads=b_src_list, writes=[b_lq2P], writes_append=[b_ln2P])
        op("dve", lambda e: e.tensor_scalar(sl_m, sl_s1, 1.0 / D, None,
                                            ALU.mult),
            reads=[b_ln2P], writes_append=[b_ln2P])
        op("dve", lambda e: e.tensor_scalar(sl_v, sl_s2, 1.0 / D, None,
                                            ALU.mult),
            reads=[b_ln2P], writes_append=[b_ln2P])
        op("dve", lambda e: e.scalar_tensor_tensor(
            sl_nm, sl_m, -1.0, sl_m, ALU.mult, ALU.mult),
            reads=[b_ln2P], writes_append=[b_ln2P])
        op("dve", lambda e: e.tensor_add(sl_v, sl_v, sl_nm),
            reads=[b_ln2P], writes_append=[b_ln2P])
        op("dve", lambda e: e.tensor_scalar(sl_v, sl_v, LN_EPS, None,
                                            ALU.add),
            reads=[b_ln2P], writes_append=[b_ln2P])
        op("act", lambda e: e.activation(sl_v, sl_v, AF.Ln),
            reads=[b_ln2P], writes_append=[b_ln2P])
        op("act", lambda e: e.activation(sl_r, sl_v, AF.Exp, scale=-0.5),
            reads=[b_ln2P], writes_append=[b_ln2P])
        op("dve", lambda e: e.tensor_scalar(sl_nm, sl_m, sl_r, -1.0,
                                            ALU.mult, ALU.mult),
            reads=[b_ln2P], writes_append=[b_ln2P])
        op("dve", lambda e: e.tensor_scalar(dst_ap, src_ap, sl_r, sl_nm,
                                            ALU.mult, ALU.add),
            reads=b_src_list + [b_ln2P], writes=[b_dst])

    for tc in range(NT):
        sl = tc % 2
        emit_ln2(x_post[:, tc * D:(tc + 1) * D], [b_xp[tc]],
                 xn2_tm[:, 0:D], b_xn2tm[sl], par=sl)
        pst, b_pst = ((ps_t, b_pt) if sl == 0 else (ps_t2, b_pt2))
        for dc in range(8):
            op("pe", lambda e, sl=sl, dc=dc, pst=pst: e.transpose(
                pst[:, dc * 128:(dc + 1) * 128],
                xn2_tm[:, dc * 128:(dc + 1) * 128],
                ident[:]),
                reads=[b_xn2tm[sl], b_ident],
                writes=[b_pst] if dc == 0 else (),
                writes_append=[b_pst] if dc > 0 else ())
        for dc in range(8):
            op("dve", lambda e, tc=tc, dc=dc, pst=pst: e.tensor_copy(
                xn2[:, dc * T + tc * 128:dc * T + (tc + 1) * 128],
                pst[:, dc * 128:(dc + 1) * 128]),
                reads=[b_pst], writes_append=[b_xn2[dc]])

        for dc in range(8):
            op("pe", lambda e, tc=tc, dc=dc: e.matmul(
                ps_c[:, 0:NE],
                xn2[:, dc * T + tc * 128:dc * T + (tc + 1) * 128],
                wrt[:, dc * NE:(dc + 1) * NE],
                start=(dc == 0), stop=(dc == 7)),
                reads=[b_xn2[dc], b_wrt],
                writes=[b_pc] if dc == 0 else (),
                writes_append=[b_pc] if dc > 0 else ())
        gt = gts[:, tc * NE:(tc + 1) * NE]
        op("dve", lambda e: e.reduce_max(r1a[:, 0:1], ps_c[:, 0:NE],
                                         axis=AX.X),
            reads=[b_pc], writes=[b_r1a])
        op("dve", lambda e: e.tensor_scalar(r1a[:, 1:2], r1a[:, 0:1], -1.0,
                                            None, ALU.mult),
            reads=[b_r1a], writes_append=[b_r1a])
        op("act", lambda e, gt=gt: e.activation(gt, ps_c[:, 0:NE], AF.Exp,
                                                bias=r1a[:, 1:2]),
            reads=[b_pc, b_r1a], writes=[b_gts[tc]])
        op("dve", lambda e, gt=gt: e.reduce_sum(r1a[:, 2:3], gt, axis=AX.X),
            reads=[b_gts[tc]], writes_append=[b_r1a])
        op("dve", lambda e: e.reciprocal(r1a[:, 3:4], r1a[:, 2:3]),
            reads=[b_r1a], writes_append=[b_r1a])
        op("dve", lambda e, gt=gt: e.tensor_scalar(gt, gt, r1a[:, 3:4], None,
                                                   ALU.mult),
            reads=[b_r1a], writes=[b_gts[tc]])
        op("dve", lambda e, gt=gt: e.max(t8[:], gt),
            reads=[b_gts[tc]], writes=[b_t8])
        op("dve", lambda e, gt=gt: e.max_index(i8u[:], t8[:], gt),
            reads=[b_gts[tc], b_t8], writes=[b_i8u])
        op("dve", lambda e, tc=tc: e.tensor_copy(i12[:, tc * 2:tc * 2 + 2],
                                                 i8u[:, 0:2]),
            reads=[b_i8u], writes=[b_i12[tc]])
        op("dve", lambda e, tc=tc: e.tensor_copy(g12[:, tc * 2:tc * 2 + 2],
                                                 t8[:, 0:2]),
            reads=[b_t8], writes=[b_g12[tc]])
        mk = msk[:, tc * NE:(tc + 1) * NE]
        op("dve", lambda e, tc=tc: e.tensor_scalar(
            r8a[:], iota8[:], i12[:, tc * 2:tc * 2 + 1], None, ALU.is_equal),
            reads=[b_iota8, b_i12[tc]], writes=[b_r8a])
        op("dve", lambda e, tc=tc: e.tensor_scalar(
            r8b[:], iota8[:], i12[:, tc * 2 + 1:tc * 2 + 2], None,
            ALU.is_equal),
            reads=[b_iota8, b_i12[tc]], writes=[b_r8b])
        op("dve", lambda e, mk=mk: e.tensor_tensor(mk, r8a[:], r8b[:],
                                                   ALU.add),
            reads=[b_r8a, b_r8b], writes=[b_msk[tc]])
        op("pe", lambda e, mk=mk, tc=tc: e.matmul(
            ps_o[0:NE, tc:tc + 1], mk, onescol[:],
            start=True, stop=True),
            reads=[b_msk[tc], b_onescol],
            writes=[b_po] if tc == 0 else (),
            writes_append=[b_po] if tc > 0 else ())
    op("dve", lambda e: e.tensor_copy(cs_s[:], ps_o[0:NE, 0:NT]),
        reads=[b_po], writes=[b_css])
    op("dve", lambda e: e.reduce_sum(cnt_s[:], cs_s[:], axis=AX.X),
        reads=[b_css], writes=[b_cnts])
    op("sync", lambda e: e.dma_start(
        out=cc_ci.rearrange("o e -> e o"), in_=cnt_s[:]),
        dma=True, reads=[b_cnts], writes=[b_ccc_i])
    ev_cc = op("pool", lambda e: e.collective_compute(
        "AllGather", ALU.bypass,
        replica_groups=[[0, 1, 2, 3, 4, 5, 6, 7]],
        ins=[cc_ci.opt()], outs=[cc_cg.opt()]),
        cc=True, reads=[b_ccc_i], writes=[b_ccc_o])
    op("sync", lambda e: e.dma_start(out=cnt_g[:], in_=cc_cg),
        dma=True, reads=[b_ccc_o], writes=[b_cntg])
    op("sync", lambda e: e.dma_start(out=pm_s[:], in_=prefm), dma=True,
       writes=[b_pms])
    op("pe", lambda e: e.matmul(ps_c[0:NE, 0:1], cnt_g[:], pm_s[:],
                                start=True, stop=True),
        reads=[b_cntg, b_pms], writes=[b_pc])
    op("dve", lambda e: e.tensor_copy(offs_s[:], ps_c[0:NE, 0:1]),
        reads=[b_pc], writes=[b_offs])
    op("pe", lambda e: e.transpose(ps_c[0:1, 0:NE], offs_s[:],
                                   ident32[0:NE, 0:NE]),
        reads=[b_offs, b_ident32], writes=[b_pc])
    op("dve", lambda e: e.tensor_copy(offsr[:], ps_c[0:1, 0:NE]),
        reads=[b_pc], writes=[b_offsr])
    op("pe", lambda e: e.transpose(ps_c[0:NT, 0:NE], cs_s[:],
                                   ident32[0:NE, 0:NE]),
        reads=[b_css, b_ident32], writes=[b_pc])
    op("dve", lambda e: e.tensor_copy(cst[:], ps_c[0:NT, 0:NE]),
        reads=[b_pc], writes=[b_cst])
    op("pe", lambda e: e.matmul(ps_c[0:NT, 0:NE], sl16[:], cst[:],
                                start=True, stop=False),
        reads=[b_sl16, b_cst], writes=[b_pc])
    op("pe", lambda e: e.matmul(ps_c[0:NT, 0:NE], onesr32[:, 0:NT], offsr[:],
                                start=False, stop=True),
        reads=[b_ones, b_offsr], writes_append=[b_pc])
    op("dve", lambda e: e.tensor_copy(carr_st[:], ps_c[0:NT, 0:NE]),
        reads=[b_pc], writes=[b_carrst])
    op("sync", lambda e: e.dma_start(out=carr_d, in_=carr_st[:]),
        dma=True, reads=[b_carrst], writes=[b_carr_d])
    op("sync", lambda e: e.dma_start(
        out=carrf[:], in_=carr_d.rearrange("t e -> (t e)").unsqueeze(0)),
        dma=True, reads=[b_carr_d], writes=[b_carrf])

    for tc in range(NT):
        mk = msk[:, tc * NE:(tc + 1) * NE]
        op("pe", lambda e, mk=mk: e.matmul(
            ps_c[:, 0:NE], tri_incl[:], mk, start=True, stop=False),
            reads=[b_tri, b_msk[tc]], writes=[b_pc])
        op("pe", lambda e, tc=tc: e.matmul(
            ps_c[:, 0:NE], onesr32[:, 0:128],
            carrf[:, tc * NE:(tc + 1) * NE], start=False, stop=True),
            reads=[b_ones, b_carrf], writes_append=[b_pc])
        op("dve", lambda e: e.tensor_scalar(
            r8a[:], ps_c[:, 0:NE], CAP + 0.5, None, ALU.is_lt),
            reads=[b_pc], writes=[b_r8a])
        op("dve", lambda e, mk=mk: e.tensor_tensor(r8a[:], r8a[:], mk,
                                                   ALU.mult),
            reads=[b_msk[tc]], writes=[b_r8a])
        op("dve", lambda e, tc=tc: e.tensor_scalar(
            r8b[:], iota8[:], i12[:, tc * 2:tc * 2 + 1], None, ALU.is_equal),
            reads=[b_iota8, b_i12[tc]], writes=[b_r8b])
        op("dve", lambda e: e.scalar_tensor_tensor(
            r8b[:], r8b[:], 1.0, r8a[:], ALU.mult, ALU.mult,
            accum_out=r1a[:, 0:1]),
            reads=[b_r8a], writes=[b_r8b], writes_append=[b_r1a])
        op("dve", lambda e, tc=tc: e.tensor_scalar(
            r8b[:], iota8[:], i12[:, tc * 2 + 1:tc * 2 + 2], None,
            ALU.is_equal),
            reads=[b_iota8, b_i12[tc]], writes=[b_r8b])
        op("dve", lambda e: e.scalar_tensor_tensor(
            r8b[:], r8b[:], 1.0, r8a[:], ALU.mult, ALU.mult,
            accum_out=r1a[:, 1:2]),
            reads=[b_r8a], writes=[b_r8b], writes_append=[b_r1a])
        op("dve", lambda e, tc=tc: e.tensor_tensor(
            r1a[:, 2:3], g12[:, tc * 2:tc * 2 + 1], r1a[:, 0:1], ALU.mult),
            reads=[b_g12[tc], b_r1a], writes_append=[b_r1a])
        op("dve", lambda e, tc=tc: e.tensor_tensor(
            r1a[:, 3:4], g12[:, tc * 2 + 1:tc * 2 + 2], r1a[:, 1:2],
            ALU.mult),
            reads=[b_g12[tc], b_r1a], writes_append=[b_r1a])
        op("dve", lambda e: e.tensor_tensor(
            r1a[:, 4:5], r1a[:, 2:3], r1a[:, 3:4], ALU.add),
            reads=[b_r1a], writes_append=[b_r1a])
        op("dve", lambda e: e.tensor_scalar(
            r1a[:, 4:5], r1a[:, 4:5], MOE_EPS, None, ALU.add),
            reads=[b_r1a], writes_append=[b_r1a])
        op("dve", lambda e: e.reciprocal(r1a[:, 5:6], r1a[:, 4:5]),
            reads=[b_r1a], writes_append=[b_r1a])
        op("dve", lambda e: e.tensor_tensor(
            r1a[:, 2:3], r1a[:, 2:3], r1a[:, 5:6], ALU.mult),
            reads=[b_r1a], writes_append=[b_r1a])
        op("dve", lambda e: e.tensor_tensor(
            r1a[:, 3:4], r1a[:, 3:4], r1a[:, 5:6], ALU.mult),
            reads=[b_r1a], writes_append=[b_r1a])
        op("dve", lambda e, tc=tc: e.tensor_scalar(
            r8a[:], iota8[:], i12[:, tc * 2:tc * 2 + 1], None, ALU.is_equal),
            reads=[b_iota8, b_i12[tc]], writes=[b_r8a])
        op("dve", lambda e: e.tensor_scalar(
            r8a[:], r8a[:], r1a[:, 2:3], None, ALU.mult),
            reads=[b_r1a], writes=[b_r8a])
        op("dve", lambda e, tc=tc: e.tensor_scalar(
            r8b[:], iota8[:], i12[:, tc * 2 + 1:tc * 2 + 2], None,
            ALU.is_equal),
            reads=[b_iota8, b_i12[tc]], writes=[b_r8b])
        op("dve", lambda e: e.tensor_scalar(
            r8b[:], r8b[:], r1a[:, 3:4], None, ALU.mult),
            reads=[b_r1a], writes=[b_r8b])
        op("dve", lambda e, tc=tc: e.tensor_tensor(
            comb[:, tc * NE:(tc + 1) * NE], r8a[:], r8b[:], ALU.add),
            reads=[b_r8a, b_r8b], writes=[b_comb[tc]])

    dump("xn2", xn2[:], b_xn2, BF16)
    dump("gts", gts[:], b_gts)
    dump("msk", msk[:], b_msk)
    dump("i12", i12[:], b_i12)
    dump("g12", g12[:], b_g12)
    dump("comb", comb[:], b_comb)
    dump("carrf", carrf[:], [b_carrf])
    dump("cntg", cnt_g[:], [b_cntg])
    pst_ctx.close()
    s.barrier()

    ps_d = ctx.enter_context(nc.psum_tensor("ps_d", [128, 512], F32))
    ps_e = ctx.enter_context(nc.psum_tensor("ps_e", [128, 512], F32))
    b_pd, b_pe_ = Buf("pd"), Buf("pe_")
    DBANKS = BANKS + [(ps_o[:], b_po), (ps_c[:], b_pc), (ps_d[:], b_pd),
                      (ps_e[:], b_pe_)]

    c_fm = CSB("c_fm", [NE, T], BF16)
    h_sb = CSB("h_sb", [128, 16 * 512], BF16)
    cbc = cdctx.enter_context(
        nc.sbuf_tensor("cbc", [128, NE * 512], BF16, side="right"))
    he = cdctx.enter_context(
        nc.sbuf_tensor("he", [128, 2 * 16 * 512], BF16, side="right"))
    wd_rhs = cdctx.enter_context(
        nc.sbuf_tensor("wd_rhs", [128, 3 * 1024], BF16, side="right"))
    b_cfm = Buf()
    b_h = [Buf() for _ in range(16)]
    b_cbc = [Buf() for _ in range(NE)]
    b_he = [Buf(), Buf()]
    b_wdr = [Buf(), Buf(), Buf()]

    for tc in range(NT):
        pp, b_pp = (ps_d, b_pd) if tc % 2 == 0 else (ps_e, b_pe_)
        op("pe", lambda e, tc=tc, pp=pp: e.transpose(
            pp[0:NE, 0:128], comb[:, tc * NE:(tc + 1) * NE], ident32[:]),
            reads=[b_comb[tc], b_ident32], writes=[b_pp])
        op("dve", lambda e, tc=tc, pp=pp: e.tensor_copy(
            c_fm[:, tc * 128:(tc + 1) * 128], pp[0:NE, 0:128]),
            reads=[b_pp], writes_append=[b_cfm])

    dump("cfm", c_fm[:], [b_cfm], BF16)
    dump("wdt", wdt_d, [b_wdt_d], BF16)
    ev_y = []
    for tsb in range(4):
        for hc in range(16):
            pp, b_pp = (ps_o, b_po) if hc % 2 == 0 else (ps_c, b_pc)
            for dc in range(8):
                op("pe", lambda e, tsb=tsb, hc=hc, dc=dc, pp=pp: e.matmul(
                    pp[:, 0:512],
                    wgt_sb[:, dc * DH + hc * 128:dc * DH + (hc + 1) * 128],
                    xn2[:, dc * T + tsb * 512:dc * T + (tsb + 1) * 512],
                    start=(dc == 0), stop=(dc == 7)),
                    reads=[b_wgt_sb, b_xn2[dc]],
                    writes=[b_pp] if dc == 0 else (),
                    writes_append=[b_pp] if dc > 0 else ())
            op("act", lambda e, hc=hc, pp=pp: e.activation(
                h_sb[:, hc * 512:(hc + 1) * 512], pp[:, 0:512], AF.Silu),
                reads=[b_pp], writes=[b_h[hc]])
        if tsb == 0:
            dump("h0", h_sb[:], b_h, BF16)
        for ei in range(NE):
            pp, b_pp = (ps_o, b_po) if ei % 2 == 0 else (ps_c, b_pc)
            op("pe", lambda e, tsb=tsb, ei=ei, pp=pp: e.matmul(
                pp[:, 0:512], sel8[:, ei * 128:(ei + 1) * 128],
                c_fm[:, tsb * 512:(tsb + 1) * 512],
                start=True, stop=True),
                reads=[b_sel8, b_cfm], writes=[b_pp])
            op("dve", lambda e, ei=ei, pp=pp: e.tensor_copy(
                cbc[:, ei * 512:(ei + 1) * 512], pp[:, 0:512]),
                reads=[b_pp], writes=[b_cbc[ei]])
        for ei in range(NE):
            esl = ei % 2
            for hc in range(16):
                op("dve", lambda e, ei=ei, esl=esl, hc=hc: e.tensor_tensor(
                    he[:, esl * 8192 + hc * 512:esl * 8192 + (hc + 1) * 512],
                    h_sb[:, hc * 512:(hc + 1) * 512],
                    cbc[:, ei * 512:(ei + 1) * 512], ALU.mult),
                    reads=[b_h[hc], b_cbc[ei]],
                    writes=[b_he[esl]] if hc == 0 else (),
                    writes_append=[b_he[esl]] if hc > 0 else ())
            if tsb == 0 and ei == 0:
                dump("he0", he[:, 0:8192], [b_he[0]], BF16)
            for hc in range(16):
                wsl = hc % 3
                op("sync", lambda e, tsb=tsb, ei=ei, hc=hc, wsl=wsl:
                    e.dma_start(
                        out=wd_rhs[:, wsl * 1024:(wsl + 1) * 1024],
                        in_=wdt_d[(ei * 16 + hc) * 128:
                                  (ei * 16 + hc + 1) * 128, :]),
                    dma=True, reads=[b_wdt_d], writes=[b_wdr[wsl]])
                for tcl in range(4):
                    for dh2 in range(2):
                        pb, b_pb = DBANKS[tcl * 2 + dh2]
                        first = (ei == 0 and hc == 0)
                        last = (ei == NE - 1 and hc == 15)
                        op("pe", lambda e, ei=ei, esl=esl, hc=hc, tcl=tcl,
                            dh2=dh2, wsl=wsl, pb=pb, first=first, last=last:
                            e.matmul(
                                pb,
                                he[:, esl * 8192 + hc * 512 + tcl * 128:
                                   esl * 8192 + hc * 512 + (tcl + 1) * 128],
                                wd_rhs[:, wsl * 1024 + dh2 * 512:
                                       wsl * 1024 + (dh2 + 1) * 512],
                                start=first, stop=last),
                            reads=[b_he[esl], b_wdr[wsl]],
                            writes=[b_pb] if first else (),
                            writes_append=[b_pb] if not first else ())
        for tcl in range(4):
            tc = tsb * 4 + tcl
            for dh2 in range(2):
                pb, b_pb = DBANKS[tcl * 2 + dh2]
                op("dve", lambda e, tc=tc, dh2=dh2, pb=pb: e.tensor_tensor(
                    x_post[:, tc * D + dh2 * 512:tc * D + (dh2 + 1) * 512],
                    x_post[:, tc * D + dh2 * 512:tc * D + (dh2 + 1) * 512],
                    pb, ALU.add),
                    reads=[b_pb, b_xp[tc]], writes=[b_xp[tc]])
            ev = op("sync", lambda e, tc=tc: e.dma_start(
                out=y_out[tc * 128:(tc + 1) * 128, :],
                in_=x_post[:, tc * D:(tc + 1) * D]),
                dma=True, reads=[b_xp[tc]])
            ev_y.append(ev)

    op("sync", lambda e: e.nop(), waits=ev_y + [ev_k, ev_v, ev_cc])
    s.barrier()
    op("sync", lambda e: e.nop())

    section, semstack = s.emit(nc)
    with nc.Block() as block:
        @block.sync
        def _(e):
            section(e, "sync")

        @block.gpsimd
        def _(e):
            section(e, "pool")

        @block.tensor
        def _(e):
            section(e, "pe")

        @block.vector
        def _(e):
            section(e, "dve")

        @block.scalar
        def _(e):
            section(e, "act")

    semstack.close()
    cdctx.close()
    ctx.close()
    return nc


def kernel(L0_attn_x, L0_attn_Wqkv, L0_attn_Wo, L0_attn_bo,
           L0_ffn_w_router, L0_ffn_w_gate, L0_ffn_w_down):
    x = np.ascontiguousarray(np.asarray(L0_attn_x, dtype=np.float32))
    wqkv = np.ascontiguousarray(np.asarray(L0_attn_Wqkv, dtype=np.float32))
    wo = np.ascontiguousarray(np.asarray(L0_attn_Wo, dtype=np.float32))
    bo = np.ascontiguousarray(
        np.asarray(L0_attn_bo, dtype=np.float32).reshape(1, D))
    wr = np.ascontiguousarray(np.asarray(L0_ffn_w_router, dtype=np.float32))
    wg = np.ascontiguousarray(np.asarray(L0_ffn_w_gate, dtype=np.float32))
    wd = np.ascontiguousarray(
        np.asarray(L0_ffn_w_down, dtype=np.float32).reshape(NE * D, DH))

    nc = build_program()
    in_maps = []
    for c in range(8):
        bc, sc = c // 4, c % 4
        xs = np.ascontiguousarray(x[bc, sc * T:(sc + 1) * T, :])
        posv = np.full((128, 1), float(sc * T), dtype=np.float32)
        prefm = np.array([[1.0] if r < c else [0.0] for r in range(NE)],
                         dtype=np.float32)
        in_maps.append({
            "x_in": xs, "wqkv": wqkv, "wo": wo, "bo": bo,
            "wrout": wr, "wgate": wg, "wdown": wd,
            "posv": posv, "prefm": prefm,
        })
    res = run_bass_kernel_spmd(nc, in_maps, core_ids=list(range(8)))
    global LAST_RESULTS
    LAST_RESULTS = res.results
    out = np.empty((2, TK, D), dtype=np.float32)
    for c in range(8):
        bc, sc = c // 4, c % 4
        out[bc, sc * T:(sc + 1) * T, :] = res.results[c]["y_out"]
    return out


# version 14
# speedup vs baseline: 1.1449x; 1.1449x over previous
import numpy as np
import ml_dtypes
from contextlib import ExitStack

import concourse.bass as bass
import concourse.mybir as mybir
from concourse.bass_utils import run_bass_kernel_spmd

F32 = mybir.dt.float32
BF16 = mybir.dt.bfloat16
U32 = mybir.dt.uint32
I32 = mybir.dt.int32
AF = mybir.ActivationFunctionType
ALU = mybir.AluOpType
AX = mybir.AxisListType

T = 2048
TK = 8192
D = 1024
H = 16
HD = 64
DH = 2048
NE = 8
NT = T // 128
CAP = 2560.0
LN_EPS = 1e-5
QK_EPS = 1e-6
MOE_EPS = 1e-6

DEBUG = []
SEQUENTIAL = False
STRICT = {"dve"}
LAST_RESULTS = None

ENGINES = ("pe", "act", "dve", "pool", "sync")
STREAMS = ("pe", "act", "dve", "pool", "sync", "d_sync", "d_pool", "cc")


class Ev:
    __slots__ = ("stream", "idx", "snap")

    def __init__(self, stream, idx, snap):
        self.stream, self.idx, self.snap = stream, idx, snap


class Buf:
    __slots__ = ("name", "writes", "readers")

    def __init__(self, name=""):
        self.name, self.writes, self.readers = name, [], []


class Op:
    __slots__ = ("eng", "fn", "stream", "idx", "waits", "ev", "marked", "inc")


def _merge(dst, src):
    for k, v in src.items():
        if dst.get(k, -1) < v:
            dst[k] = v


class Sched:
    def __init__(self):
        self.ops = []
        self.n = {s: 0 for s in STREAMS}
        self.clock = {e: {} for e in ENGINES}
        self.pending = {e: [] for e in ENGINES}

    def barrier(self):
        last = {}
        for o in self.ops:
            last[o.stream] = o
        evs = [o.ev for o in last.values()]
        for e in ENGINES:
            self.pending[e] = list(evs)

    def op(self, eng, fn, reads=(), writes=(), writes_append=(), waits=(),
           dma=False, cc=False):
        stream = "cc" if cc else (("d_" + eng) if dma else eng)
        inc = 16 if dma else 1
        need = list(waits)
        if SEQUENTIAL and self.ops:
            need.append(self.ops[-1].ev)
        for b in reads:
            need += b.writes
        for b in writes:
            need += b.writes + b.readers
        for b in writes_append:
            need += b.readers
        need += self.pending[eng]
        self.pending[eng] = []
        clk = self.clock[eng]
        by_stream = {}
        for ev in need:
            if clk.get(ev.stream, -1) >= ev.idx:
                continue
            cur = by_stream.get(ev.stream)
            if cur is None or cur.idx < ev.idx:
                by_stream[ev.stream] = ev
        emit_waits = list(by_stream.values())
        for ev in emit_waits:
            _merge(clk, ev.snap)
        idx = self.n[stream]
        self.n[stream] += 1
        snap = dict(clk)
        if stream == eng:
            if eng not in STRICT:
                clk[eng] = idx
            snap[eng] = idx
        else:
            snap[eng] = self.n[eng] - 1
            snap[stream] = idx
        ev = Ev(stream, idx, snap)
        o = Op()
        o.eng, o.fn, o.stream, o.idx = eng, fn, stream, idx
        o.waits, o.ev, o.marked, o.inc = emit_waits, ev, False, inc
        self.ops.append(o)
        for b in reads:
            b.readers.append(ev)
        for b in writes:
            b.writes = [ev]
            b.readers = []
        for b in writes_append:
            b.writes.append(ev)
        return ev

    def emit(self, nc):
        lastop = {}
        for o in self.ops:
            lastop[(o.stream, o.idx)] = o
            if o.stream.startswith("d_") or o.stream == "cc":
                o.marked = True
        for o in self.ops:
            for ev in o.waits:
                lastop[(ev.stream, ev.idx)].marked = True
        last_stream = {}
        for o in self.ops:
            last_stream[o.stream] = o
        for o in last_stream.values():
            o.marked = True
        cum = {s: 0 for s in STREAMS}
        semval = {}
        for o in self.ops:
            if o.marked:
                cum[o.stream] += o.inc
                semval[(o.stream, o.idx)] = cum[o.stream]
        stack = ExitStack()
        sems = {s: stack.enter_context(nc.semaphore("sem_" + s))
                for s in STREAMS if cum[s] > 0}

        def section(engobj, ename):
            for o in self.ops:
                if o.eng != ename:
                    continue
                for ev in o.waits:
                    engobj.wait_ge(sems[ev.stream], semval[(ev.stream, ev.idx)])
                inst = o.fn(engobj)
                if o.marked:
                    inst.then_inc(sems[o.stream], o.inc)

        return section, stack


def build_program():
    nc = bass.Bass("TRN2", target_bir_lowering=False)
    s = Sched()
    op = s.op

    x_in = nc.dram_tensor("x_in", [T, D], F32, kind="ExternalInput").ap()
    wqkv = nc.dram_tensor("wqkv", [D, 3 * D], F32, kind="ExternalInput").ap()
    wo_d = nc.dram_tensor("wo", [D, D], F32, kind="ExternalInput").ap()
    bo_d = nc.dram_tensor("bo", [1, D], F32, kind="ExternalInput").ap()
    wr_d = nc.dram_tensor("wrout", [NE, D], F32, kind="ExternalInput").ap()
    wg_d = nc.dram_tensor("wgate", [DH, D], F32, kind="ExternalInput").ap()
    wd_d = nc.dram_tensor("wdown", [NE * D, DH], F32, kind="ExternalInput").ap()
    posv = nc.dram_tensor("posv", [128, 1], F32, kind="ExternalInput").ap()
    prefm = nc.dram_tensor("prefm", [NE, 1], F32, kind="ExternalInput").ap()
    y_out = nc.dram_tensor("y_out", [T, D], F32, kind="ExternalOutput").ap()

    dbg_outs = {}
    def dbg(name, shape, dt=F32):
        if name in DEBUG and name not in dbg_outs:
            dbg_outs[name] = nc.dram_tensor("dbg_" + name, shape, dt,
                                            kind="ExternalOutput").ap()
        return dbg_outs.get(name)

    def dump(name, ap, bufs, dt=F32):
        d = dbg(name, list(ap.shape), dt)
        if d is not None:
            s.op("sync", lambda e, d=d, ap=ap: e.dma_start(out=d, in_=ap),
                 dma=True, reads=list(bufs))

    cc_ki = nc.dram_tensor("cc_ki", [D, T], BF16, kind="Internal").ap()
    cc_vi = nc.dram_tensor("cc_vi", [T, D], BF16, kind="Internal").ap()
    cc_kg = nc.dram_tensor("cc_kg", [4 * D, T], BF16, kind="Internal").ap()
    cc_vg = nc.dram_tensor("cc_vg", [TK, D], BF16, kind="Internal").ap()
    cc_ci = nc.dram_tensor("cc_ci", [1, NE], F32, kind="Internal").ap()
    cc_cg = nc.dram_tensor("cc_cg", [NE, NE], F32, kind="Internal").ap()
    wdt_d = nc.dram_tensor("wdt_d", [NE * 16 * 128, 8 * 128], BF16,
                           kind="Internal").ap()
    wgt_d = nc.dram_tensor("wgt_d", [8 * 128, 16 * 128], BF16,
                           kind="Internal").ap()
    carr_d = nc.dram_tensor("carr_d", [NT, NE], F32, kind="Internal").ap()
    b_cck, b_ccv, b_ccko, b_ccvo = Buf(), Buf(), Buf(), Buf()
    b_ccc_i, b_ccc_o = Buf(), Buf()
    b_wdt_d, b_wgt_d, b_carr_d = Buf(), Buf(), Buf()

    ctx = ExitStack()
    SB = lambda nm, shp, dt: ctx.enter_context(nc.sbuf_tensor(nm, shp, dt))
    PS = lambda nm, shp, dt: ctx.enter_context(nc.psum_tensor(nm, shp, dt))

    ps_a = PS("ps_a", [128, 2048], F32)
    ps_o = PS("ps_o", [128, 512], F32)
    ps_c = PS("ps_c", [128, 512], F32)
    pst_ctx = ExitStack()
    ps_t = pst_ctx.enter_context(nc.psum_tensor("ps_t", [128, 1024], BF16))
    ps_t2 = pst_ctx.enter_context(nc.psum_tensor("ps_t2", [128, 1024], BF16))
    b_a0, b_a1, b_b0, b_b1 = Buf("a0"), Buf("a1"), Buf("a2"), Buf("a3")
    b_po, b_pc, b_pt, b_pt2 = Buf("po"), Buf("pc"), Buf("pt"), Buf("pt2")
    A0 = ps_a[:, 0:512]
    A1 = ps_a[:, 512:1024]
    B0 = ps_a[:, 1024:1536]
    B1 = ps_a[:, 1536:2048]
    BANKS = [(A0, b_a0), (A1, b_a1), (B0, b_b0), (B1, b_b1)]
    BANKBUFS = [b_a0, b_a1, b_b0, b_b1]

    ident = SB("ident", [128, 128], BF16)
    ident32 = SB("ident32", [128, 128], F32)
    onesr32 = SB("onesr32", [1, 128], F32)
    onescol = SB("onescol", [128, 1], F32)
    tri_incl = SB("tri_incl", [128, 128], F32)
    sl16 = SB("sl16", [16, 16], F32)
    iota8 = SB("iota8", [128, NE], F32)
    sel8 = SB("sel8", [NE, NE * 128], BF16)
    wrt = SB("wrt", [128, 8 * NE], BF16)
    wgt_sb = SB("wgt_sb", [128, 8 * DH], BF16)
    comb = SB("comb", [128, NT * NE], F32)
    b_ident, b_ident32, b_ones, b_onescol = Buf(), Buf(), Buf(), Buf()
    b_tri, b_sl16, b_iota8, b_sel8 = Buf(), Buf(), Buf(), Buf()
    b_wrt, b_wgt_sb = Buf(), Buf()
    b_comb = [Buf() for _ in range(NT)]

    scope_attn = ExitStack()
    attn_fm = scope_attn.enter_context(
        nc.sbuf_tensor("attn_fm", [128, 8 * T], BF16, side="right"))
    scope_q = ExitStack()
    q_fm = scope_q.enter_context(nc.sbuf_tensor("q_fm", [128, 8 * T], BF16))
    b_qfm = [Buf(f"qfm{i}") for i in range(8)]
    b_attn = [Buf(f"attn{i}") for i in range(8)]

    for t_, b_ in ((ident, b_ident), (ident32, b_ident32)):
        op("pool", lambda e, t_=t_: e.memset(t_[:], 0.0), writes=[b_])
        op("pool", lambda e, t_=t_: e.affine_select(
            out=t_[:], in_=t_[:], compare_op=ALU.not_equal, fill=1.0,
            base=0, pattern=[[-1, 128]], channel_multiplier=1),
            reads=[b_], writes=[b_])
    op("pool", lambda e: e.memset(onesr32[:], 1.0), writes=[b_ones])
    op("pool", lambda e: e.memset(onescol[:], 1.0), writes=[b_onescol])
    op("pool", lambda e: e.memset(tri_incl[:], 1.0), writes=[b_tri])
    op("pool", lambda e: e.affine_select(
        out=tri_incl[:], in_=tri_incl[:], compare_op=ALU.is_ge, fill=0.0,
        base=0, pattern=[[1, 128]], channel_multiplier=-1),
        reads=[b_tri], writes=[b_tri])
    op("pool", lambda e: e.memset(sl16[:], 1.0), writes=[b_sl16])
    op("pool", lambda e: e.affine_select(
        out=sl16[:], in_=sl16[:], compare_op=ALU.is_gt, fill=0.0,
        base=0, pattern=[[1, 16]], channel_multiplier=-1),
        reads=[b_sl16], writes=[b_sl16])
    op("pool", lambda e: e.iota(iota8[:], pattern=[[1, NE]], base=0,
                                channel_multiplier=0,
                                allow_small_or_imprecise_dtypes=True),
        writes=[b_iota8])
    op("pool", lambda e: e.memset(sel8[:], 0.0), writes=[b_sel8])
    for ei in range(NE):
        op("pool", lambda e, ei=ei: e.affine_select(
            out=sel8[:, ei * 128:(ei + 1) * 128],
            in_=sel8[:, ei * 128:(ei + 1) * 128],
            compare_op=ALU.not_equal, fill=1.0, base=-ei,
            pattern=[[0, 128]], channel_multiplier=1),
            reads=[b_sel8], writes_append=[b_sel8])

    rctx = ExitStack()
    RSB = lambda nm, shp, dt: rctx.enter_context(nc.sbuf_tensor(nm, shp, dt))
    ctab = RSB("ctab", [128, NT * 64], F32)
    stab = RSB("stab", [128, NT * 64], F32)
    b_ctab = [Buf() for _ in range(NT)]
    b_stab = [Buf() for _ in range(NT)]
    r_pos = RSB("r_pos", [128, 1], F32)
    r_invf = RSB("r_invf", [128, 32], F32)
    r_th = RSB("r_th", [128, 32], F32)
    r_u = RSB("r_u", [128, 32], F32)
    r_ni = RSB("r_ni", [128, 32], I32)
    r_n = RSB("r_n", [128, 32], F32)
    r_red = RSB("r_red", [128, 32], F32)
    r_sc = RSB("r_sc", [128, 32], F32)
    b_pos, b_invf, b_th, b_u = Buf(), Buf(), Buf(), Buf()
    b_ni, b_n, b_red, b_rsc = Buf(), Buf(), Buf(), Buf()

    op("sync", lambda e: e.dma_start(out=r_pos[:], in_=posv), dma=True,
       writes=[b_pos])
    op("pool", lambda e: e.iota(r_invf[:], pattern=[[1, 32]], base=0,
                                channel_multiplier=0,
                                allow_small_or_imprecise_dtypes=True),
        writes=[b_invf])
    op("act", lambda e: e.activation(r_invf[:], r_invf[:], AF.Exp,
                                     scale=float(-np.log(10000.0) / 32.0)),
        reads=[b_invf], writes=[b_invf])
    TWO_PI = float(2.0 * np.pi)
    C1 = 6.28125
    C2 = float(2.0 * np.pi - 6.28125)
    for tc in range(NT):
        op("pool", lambda e, tc=tc: e.iota(
            r_th[:], pattern=[[0, 32]], base=tc * 128, channel_multiplier=1,
            allow_small_or_imprecise_dtypes=True), writes=[b_th])
        op("dve", lambda e: e.tensor_scalar(r_th[:], r_th[:], r_pos[:], None,
                                            ALU.add),
            reads=[b_pos], writes=[b_th])
        op("dve", lambda e: e.tensor_mul(r_th[:], r_th[:], r_invf[:]),
            reads=[b_invf], writes=[b_th])
        for half in (0, 1):
            badd = 0.0 if half == 0 else 0.25
            op("dve", lambda e, badd=badd: e.tensor_scalar(
                r_u[:], r_th[:], 1.0 / TWO_PI, badd, ALU.mult, ALU.add),
                reads=[b_th], writes=[b_u])
            op("dve", lambda e: e.tensor_copy(r_ni[:], r_u[:]),
                reads=[b_u], writes=[b_ni])
            op("dve", lambda e: e.tensor_copy(r_n[:], r_ni[:]),
                reads=[b_ni], writes=[b_n])
            op("dve", lambda e: e.scalar_tensor_tensor(
                r_red[:], r_n[:], -C1, r_th[:], ALU.mult, ALU.add),
                reads=[b_n, b_th], writes=[b_red])
            sh = 0.0 if half == 0 else float(np.pi / 2.0)
            op("dve", lambda e: e.scalar_tensor_tensor(
                r_red[:], r_n[:], -C2, r_red[:], ALU.mult, ALU.add),
                reads=[b_n], writes=[b_red])
            if half == 1:
                op("dve", lambda e, sh=sh: e.tensor_scalar(
                    r_red[:], r_red[:], sh, None, ALU.add),
                    reads=[b_red], writes=[b_red])
            op("act", lambda e: e.activation(r_sc[:], r_red[:], AF.Sin),
                reads=[b_red], writes=[b_rsc])
            if half == 0:
                op("dve", lambda e, tc=tc: e.tensor_scalar(
                    stab[:, tc * 64:tc * 64 + 32], r_sc[:], -1.0, None,
                    ALU.mult), reads=[b_rsc], writes=[b_stab[tc]])
                op("dve", lambda e, tc=tc: e.tensor_copy(
                    stab[:, tc * 64 + 32:tc * 64 + 64], r_sc[:]),
                    reads=[b_rsc], writes_append=[b_stab[tc]])
            else:
                op("dve", lambda e, tc=tc: e.tensor_copy(
                    ctab[:, tc * 64:tc * 64 + 32], r_sc[:]),
                    reads=[b_rsc], writes=[b_ctab[tc]])
                op("dve", lambda e, tc=tc: e.tensor_copy(
                    ctab[:, tc * 64 + 32:tc * 64 + 64], r_sc[:]),
                    reads=[b_rsc], writes_append=[b_ctab[tc]])

    dump("ctab0", ctab[:], b_ctab)
    dump("stab0", stab[:], b_stab)
    dump("invf", r_invf[:], [b_invf])

    wctx = ExitStack()
    WSB = lambda nm, shp, dt: wctx.enter_context(nc.sbuf_tensor(nm, shp, dt))
    wd_in = WSB("wd_in", [128, 2 * DH], BF16)
    wd_st = WSB("wd_st", [128, 2 * DH], BF16)
    wr_raw = WSB("wr_raw", [NE, D], F32)
    b_wd_in = [Buf(), Buf()]
    b_wd_st = [Buf(), Buf()]
    b_wr_raw = Buf()

    op("sync", lambda e: e.dma_start(out=wr_raw[:], in_=wr_d), dma=True,
       writes=[b_wr_raw])
    for dc in range(8):
        op("pe", lambda e, dc=dc: e.transpose(
            ps_c[:, 0:NE], wr_raw[:, dc * 128:(dc + 1) * 128],
            ident32[0:NE, 0:NE]),
            reads=[b_wr_raw, b_ident32], writes=[b_pc])
        op("dve", lambda e, dc=dc: e.tensor_copy(
            wrt[:, dc * NE:(dc + 1) * NE], ps_c[:, 0:NE]),
            reads=[b_pc], writes_append=[b_wrt])

    wdt4 = wdt_d.rearrange("(ei hc p) z -> ei hc p z", ei=NE, hc=16)
    for ei in range(NE):
        for dc in range(8):
            k = ei * 8 + dc
            sl = k % 2
            pst, b_pst = ((ps_t, b_pt) if sl == 0 else (ps_t2, b_pt2))
            op("pool", lambda e, k=k, sl=sl: e.dma_start(
                out=wd_in[:, sl * DH:(sl + 1) * DH],
                in_=wd_d[k * 128:(k + 1) * 128, :]),
                dma=True, writes=[b_wd_in[sl]])
            for hg in range(2):
                for hh in range(8):
                    hc = hg * 8 + hh
                    op("pe", lambda e, sl=sl, hc=hc, hh=hh, pst=pst: e.transpose(
                        pst[:, hh * 128:(hh + 1) * 128],
                        wd_in[:, sl * DH + hc * 128:sl * DH + (hc + 1) * 128],
                        ident[:]),
                        reads=[b_wd_in[sl], b_ident],
                        writes=[b_pst] if hh == 0 else (),
                        writes_append=[b_pst] if hh > 0 else ())
                ev_eng = "dve" if pst is ps_t else "act"
                for hh in range(8):
                    hc = hg * 8 + hh
                    if ev_eng == "dve":
                        op("dve", lambda e, sl=sl, hc=hc, hh=hh, pst=pst:
                            e.tensor_copy(
                                wd_st[:, sl * DH + hc * 128:
                                      sl * DH + (hc + 1) * 128],
                                pst[:, hh * 128:(hh + 1) * 128]),
                            reads=[b_pst], writes_append=[b_wd_st[sl]])
                    else:
                        op("act", lambda e, sl=sl, hc=hc, hh=hh, pst=pst:
                            e.copy(
                                wd_st[:, sl * DH + hc * 128:
                                      sl * DH + (hc + 1) * 128],
                                pst[:, hh * 128:(hh + 1) * 128]),
                            reads=[b_pst], writes_append=[b_wd_st[sl]])
            for hg in range(2):
                op("sync", lambda e, ei=ei, dc=dc, sl=sl, hg=hg: e.dma_start(
                    out=wdt4[ei, hg * 8:(hg + 1) * 8, :, dc * 128:(dc + 1) * 128]
                    .transpose([1, 0, 2]),
                    in_=wd_st[:, sl * DH + hg * 1024:sl * DH + (hg + 1) * 1024]
                    .rearrange("p (hc z) -> p hc z", z=128)),
                    dma=True, reads=[b_wd_st[sl]], writes_append=[b_wdt_d])

    wgt4 = wgt_d.rearrange("(dc p) z -> dc p z", dc=8)
    for hc in range(16):
        sl = hc % 2
        pst, b_pst = ((ps_t, b_pt) if sl == 0 else (ps_t2, b_pt2))
        op("pool", lambda e, hc=hc, sl=sl: e.dma_start(
            out=wd_in[:, sl * DH:sl * DH + D],
            in_=wg_d[hc * 128:(hc + 1) * 128, :]),
            dma=True, writes=[b_wd_in[sl]])
        for dc in range(8):
            op("pe", lambda e, sl=sl, dc=dc, pst=pst: e.transpose(
                pst[:, dc * 128:(dc + 1) * 128],
                wd_in[:, sl * DH + dc * 128:sl * DH + (dc + 1) * 128],
                ident[:]),
                reads=[b_wd_in[sl], b_ident],
                writes=[b_pst] if dc == 0 else (),
                writes_append=[b_pst] if dc > 0 else ())
        eng = "dve" if pst is ps_t else "act"
        for dc in range(8):
            fn = (lambda e, sl=sl, dc=dc, pst=pst: e.tensor_copy(
                wd_st[:, sl * DH + dc * 128:sl * DH + (dc + 1) * 128],
                pst[:, dc * 128:(dc + 1) * 128])) if eng == "dve" else \
                (lambda e, sl=sl, dc=dc, pst=pst: e.copy(
                    wd_st[:, sl * DH + dc * 128:sl * DH + (dc + 1) * 128],
                    pst[:, dc * 128:(dc + 1) * 128]))
            op(eng, fn, reads=[b_pst], writes_append=[b_wd_st[sl]])
        op("sync", lambda e, hc=hc, sl=sl: e.dma_start(
            out=wgt4[:, :, hc * 128:(hc + 1) * 128].transpose([1, 0, 2]),
            in_=wd_st[:, sl * DH:sl * DH + D].rearrange(
                "p (dc z) -> p dc z", z=128)),
            dma=True, reads=[b_wd_st[sl]], writes_append=[b_wgt_d])
    wctx.close()

    actx = ExitStack()
    ASB = lambda nm, shp, dt: actx.enter_context(nc.sbuf_tensor(nm, shp, dt))
    xt = ASB("xt", [128, 2 * D], F32)
    xn1 = ASB("xn1", [128, 8 * T], BF16)
    xn_tm = ASB("xn_tm", [128, 2 * D], BF16)
    wq_blk = ASB("wq_blk", [128, 2 * 8 * 512], BF16)
    qkv_blk = ASB("qkv_blk", [128, NT * 512], BF16)
    kstage = ASB("kstage", [128, 2048], BF16)
    lns = ASB("lns", [128, 16], F32)
    lnsq = ASB("lnsq", [128, 2 * D], F32)
    rp = ASB("rp", [128, 2 * 4 * 256], BF16)
    qnsq = ASB("qnsq", [128, 2 * 512], F32)
    qnss = ASB("qnss", [128, 16], F32)
    b_xt = [Buf(), Buf()]
    b_xn1 = [[Buf(f"xn1_{i}_{t}") for t in range(NT)] for i in range(8)]
    b_xn1_all = [b for row in b_xn1 for b in row]
    b_xn_tm = [Buf(), Buf()]
    b_wq = [Buf(), Buf()]
    b_qkvb = [Buf(f"qkvb{t}") for t in range(NT)]
    b_kstage = Buf()
    b_lns = [Buf(), Buf()]
    b_lnsq = [Buf(), Buf()]
    b_rp = [Buf(), Buf()]
    b_qnsq = [Buf(), Buf()]
    b_qnss = [Buf(), Buf()]

    def emit_ln(src_ap, b_src_list, dst_ap, b_dst, par=0):
        o = par * 8
        b_lnsP = b_lns[par]
        b_lnsqP = b_lnsq[par]
        lnsqv = lnsq[:, par * D:(par + 1) * D]
        sl_s1 = lns[:, o + 0:o + 1]
        sl_s2 = lns[:, o + 1:o + 2]
        sl_m = lns[:, o + 2:o + 3]
        sl_v = lns[:, o + 3:o + 4]
        sl_r = lns[:, o + 4:o + 5]
        sl_nm = lns[:, o + 5:o + 6]
        op("act", lambda e: e.activation(lnsqv, src_ap, AF.Identity,
                                         accum_out=sl_s1),
            reads=b_src_list, writes=[b_lnsqP], writes_append=[b_lnsP])
        op("act", lambda e: e.activation(lnsqv, src_ap, AF.Square,
                                         accum_out=sl_s2),
            reads=b_src_list, writes=[b_lnsqP], writes_append=[b_lnsP])
        op("dve", lambda e: e.tensor_scalar(sl_m, sl_s1, 1.0 / D, None,
                                            ALU.mult),
            reads=[b_lnsP], writes_append=[b_lnsP])
        op("dve", lambda e: e.tensor_scalar(sl_v, sl_s2, 1.0 / D, None,
                                            ALU.mult),
            reads=[b_lnsP], writes_append=[b_lnsP])
        op("dve", lambda e: e.scalar_tensor_tensor(
            sl_nm, sl_m, -1.0, sl_m, ALU.mult, ALU.mult),
            reads=[b_lnsP], writes_append=[b_lnsP])
        op("dve", lambda e: e.tensor_add(sl_v, sl_v, sl_nm),
            reads=[b_lnsP], writes_append=[b_lnsP])
        op("dve", lambda e: e.tensor_scalar(sl_v, sl_v, LN_EPS, None,
                                            ALU.add),
            reads=[b_lnsP], writes_append=[b_lnsP])
        op("act", lambda e: e.activation(sl_v, sl_v, AF.Ln),
            reads=[b_lnsP], writes_append=[b_lnsP])
        op("act", lambda e: e.activation(sl_r, sl_v, AF.Exp, scale=-0.5),
            reads=[b_lnsP], writes_append=[b_lnsP])
        op("dve", lambda e: e.tensor_scalar(sl_nm, sl_m, sl_r, -1.0,
                                            ALU.mult, ALU.mult),
            reads=[b_lnsP], writes_append=[b_lnsP])
        op("dve", lambda e: e.tensor_scalar(dst_ap, src_ap, sl_r, sl_nm,
                                            ALU.mult, ALU.add),
            reads=b_src_list + [b_lnsP], writes=[b_dst])

    for tc in range(NT):
        sl = tc % 2
        op("sync", lambda e, tc=tc, sl=sl: e.dma_start(
            out=xt[:, sl * D:(sl + 1) * D],
            in_=x_in[tc * 128:(tc + 1) * 128, :]),
            dma=True, writes=[b_xt[sl]])
        emit_ln(xt[:, sl * D:(sl + 1) * D], [b_xt[sl]],
                xn_tm[:, sl * D:(sl + 1) * D], b_xn_tm[sl], par=sl)
        pst, b_pst = ((ps_t, b_pt) if sl == 0 else (ps_t2, b_pt2))
        for dc in range(8):
            op("pe", lambda e, sl=sl, dc=dc, pst=pst: e.transpose(
                pst[:, dc * 128:(dc + 1) * 128],
                xn_tm[:, sl * D + dc * 128:sl * D + (dc + 1) * 128],
                ident[:]),
                reads=[b_xn_tm[sl], b_ident],
                writes=[b_pst] if dc == 0 else (),
                writes_append=[b_pst] if dc > 0 else ())
        eng = "dve" if pst is ps_t else "act"
        for dc in range(8):
            fn = (lambda e, tc=tc, dc=dc, pst=pst: e.tensor_copy(
                xn1[:, dc * T + tc * 128:dc * T + (tc + 1) * 128],
                pst[:, dc * 128:(dc + 1) * 128])) if eng == "dve" else \
                (lambda e, tc=tc, dc=dc, pst=pst: e.copy(
                    xn1[:, dc * T + tc * 128:dc * T + (tc + 1) * 128],
                    pst[:, dc * 128:(dc + 1) * 128]))
            op(eng, fn, reads=[b_pst], writes_append=[b_xn1[dc][tc]])

    for blk in range(6):
        sl = blk % 2
        for dc in range(8):
            op("pool", lambda e, blk=blk, sl=sl, dc=dc: e.dma_start(
                out=wq_blk[:, sl * 4096 + dc * 512:sl * 4096 + (dc + 1) * 512],
                in_=wqkv[dc * 128:(dc + 1) * 128, blk * 512:(blk + 1) * 512]),
                dma=True, writes=[b_wq[sl]] if dc == 0 else (),
                writes_append=[b_wq[sl]] if dc > 0 else ())
        for tc in range(NT):
            pbank, b_pb = BANKS[tc % 4]
            for dc in range(8):
                op("pe", lambda e, blk=blk, sl=sl, tc=tc, dc=dc, pbank=pbank:
                    e.matmul(
                        pbank,
                        xn1[:, dc * T + tc * 128:dc * T + (tc + 1) * 128],
                        wq_blk[:, sl * 4096 + dc * 512:
                               sl * 4096 + (dc + 1) * 512],
                        start=(dc == 0), stop=(dc == 7)),
                    reads=[b_xn1[dc][tc], b_wq[sl]],
                    writes=[b_pb] if dc == 0 else (),
                    writes_append=[b_pb] if dc > 0 else ())
            op("dve", lambda e, tc=tc, pbank=pbank: e.tensor_copy(
                qkv_blk[:, tc * 512:(tc + 1) * 512], pbank),
                reads=[b_pb], writes=[b_qkvb[tc]])

        if blk < 4:
            for tc in range(NT):
                base = tc * 512
                q4 = qkv_blk[:, base:base + 512].rearrange(
                    "p (h b j) -> p h b j", b=2, j=32)
                x1 = q4[:, :, 0, :]
                x2 = q4[:, :, 1, :]
                c32 = ctab[:, tc * 64:tc * 64 + 32].unsqueeze(1) \
                    .broadcast_to([128, 8, 32])
                s32n = stab[:, tc * 64:tc * 64 + 32].unsqueeze(1) \
                    .broadcast_to([128, 8, 32])
                s32p = stab[:, tc * 64 + 32:tc * 64 + 64].unsqueeze(1) \
                    .broadcast_to([128, 8, 32])
                par = tc % 2
                b_rpP = b_rp[par]
                r3 = [rp[:, par * 1024 + i * 256:par * 1024 + (i + 1) * 256]
                      .rearrange("p (h j) -> p h j", j=32) for i in range(4)]
                op("pool", lambda e, x1=x1, c32=c32, r3=r3: e.tensor_tensor(
                    r3[0], x1, c32, ALU.mult),
                    reads=[b_qkvb[tc], b_ctab[tc]], writes=[b_rpP])
                op("pool", lambda e, x2=x2, s32n=s32n, r3=r3: e.tensor_tensor(
                    r3[1], x2, s32n, ALU.mult),
                    reads=[b_qkvb[tc], b_stab[tc]], writes_append=[b_rpP])
                op("pool", lambda e, x1=x1, s32p=s32p, r3=r3: e.tensor_tensor(
                    r3[2], x1, s32p, ALU.mult),
                    reads=[b_qkvb[tc], b_stab[tc]], writes_append=[b_rpP])
                op("pool", lambda e, x2=x2, c32=c32, r3=r3: e.tensor_tensor(
                    r3[3], x2, c32, ALU.mult),
                    reads=[b_qkvb[tc], b_ctab[tc]], writes_append=[b_rpP])
                op("dve", lambda e, x1=x1, r3=r3: e.tensor_tensor(
                    x1, r3[0], r3[1], ALU.add),
                    reads=[b_rpP], writes_append=[b_qkvb[tc]])
                op("dve", lambda e, x2=x2, r3=r3: e.tensor_tensor(
                    x2, r3[2], r3[3], ALU.add),
                    reads=[b_rpP], writes_append=[b_qkvb[tc]])
                qb2 = qkv_blk[:, base:base + 512]
                qb3 = qb2.rearrange("p (h j) -> p h j", j=64)
                qnsqv = qnsq[:, par * 512:(par + 1) * 512]
                qnssv = qnss[:, par * 8:(par + 1) * 8]
                b_qnsqP, b_qnssP = b_qnsq[par], b_qnss[par]
                op("pool", lambda e, qb2=qb2, qnsqv=qnsqv: e.tensor_tensor(
                    qnsqv, qb2, qb2, ALU.mult),
                    reads=[b_qkvb[tc]], writes=[b_qnsqP])
                op("dve", lambda e, qnsqv=qnsqv, qnssv=qnssv: e.reduce_sum(
                    qnssv, qnsqv.rearrange("p (h j) -> p h j", j=64),
                    axis=AX.X),
                    reads=[b_qnsqP], writes=[b_qnssP])
                op("dve", lambda e, qnssv=qnssv: e.tensor_scalar(
                    qnssv, qnssv, QK_EPS, None, ALU.add),
                    reads=[b_qnssP], writes=[b_qnssP])
                op("act", lambda e, qnssv=qnssv: e.activation(
                    qnssv, qnssv, AF.Ln),
                    reads=[b_qnssP], writes=[b_qnssP])
                op("act", lambda e, qnssv=qnssv: e.activation(
                    qnssv, qnssv, AF.Exp, scale=-0.5),
                    reads=[b_qnssP], writes=[b_qnssP])
                rb = qnssv.unsqueeze(2).broadcast_to([128, 8, 64])
                op("dve", lambda e, qb3=qb3, rb=rb: e.tensor_tensor(
                    qb3, qb3, rb, ALU.mult),
                    reads=[b_qnssP], writes_append=[b_qkvb[tc]])

            for b2 in range(4):
                ch = blk * 4 + b2
                pst, b_pst = ((ps_t, b_pt) if b2 % 2 == 0 else (ps_t2, b_pt2))
                for tg in range(2):
                    for th in range(8):
                        tc = tg * 8 + th
                        op("pe", lambda e, tc=tc, th=th, b2=b2, pst=pst:
                            e.transpose(
                                pst[:, th * 128:(th + 1) * 128],
                                qkv_blk[:, tc * 512 + b2 * 128:
                                        tc * 512 + (b2 + 1) * 128],
                                ident[:]),
                            reads=[b_qkvb[tc], b_ident],
                            writes=[b_pst] if th == 0 else (),
                            writes_append=[b_pst] if th > 0 else ())
                    for th in range(8):
                        tc = tg * 8 + th
                        if blk < 2:
                            dst = q_fm[:, (ch % 8) * T + tc * 128:
                                       (ch % 8) * T + (tc + 1) * 128]
                            bd = b_qfm[ch % 8]
                        else:
                            dst = kstage[:, tc * 128:(tc + 1) * 128]
                            bd = b_kstage
                        if pst is ps_t:
                            op("dve", lambda e, dst=dst, th=th, pst=pst:
                                e.tensor_copy(dst,
                                              pst[:, th * 128:(th + 1) * 128]),
                                reads=[b_pst], writes_append=[bd])
                        else:
                            op("act", lambda e, dst=dst, th=th, pst=pst:
                                e.copy(dst, pst[:, th * 128:(th + 1) * 128]),
                                reads=[b_pst], writes_append=[bd])
                if blk >= 2:
                    r0 = (blk - 2) * 512 + b2 * 128
                    op("sync", lambda e, r0=r0: e.dma_start(
                        out=cc_ki[r0:r0 + 128, :], in_=kstage[:]),
                        dma=True, reads=[b_kstage], writes_append=[b_cck])
                    b_kstage.writes = []
        else:
            for tc in range(NT):
                op("sync", lambda e, blk=blk, tc=tc: e.dma_start(
                    out=cc_vi[tc * 128:(tc + 1) * 128,
                                (blk - 4) * 512:(blk - 3) * 512],
                    in_=qkv_blk[:, tc * 512:(tc + 1) * 512]),
                    dma=True, reads=[b_qkvb[tc]], writes_append=[b_ccv])

    dump("xn1", xn1[:], b_xn1_all, BF16)
    dump("qfm", q_fm[:], b_qfm, BF16)
    dump("ctab", ctab[:], b_ctab)
    dump("stab", stab[:], b_stab)
    dump("kci", cc_ki, [b_cck], BF16)
    dump("vci", cc_vi, [b_ccv], BF16)

    ev_k = op("pool", lambda e: e.collective_compute(
        "AllGather", ALU.bypass,
        replica_groups=[[0, 1, 2, 3], [4, 5, 6, 7]],
        ins=[cc_ki.opt()], outs=[cc_kg.opt()]),
        cc=True, reads=[b_cck], writes=[b_ccko])
    ev_v = op("pool", lambda e: e.collective_compute(
        "AllGather", ALU.bypass,
        replica_groups=[[0, 1, 2, 3], [4, 5, 6, 7]],
        ins=[cc_vi.opt()], outs=[cc_vg.opt()]),
        cc=True, reads=[b_ccv], writes=[b_ccvo])

    for dc in range(8):
        op("sync", lambda e, dc=dc: e.dma_start(
            out=wgt_sb[:, dc * DH:(dc + 1) * DH],
            in_=wgt4[dc]),
            dma=True, waits=[],
            reads=[b_wgt_d],
            writes=[b_wgt_sb] if dc == 0 else (),
            writes_append=[b_wgt_sb] if dc > 0 else ())

    actx.close()
    rctx.close()
    pst_ctx.close()
    s.barrier()

    bctx = ExitStack()
    ps_ob = bctx.enter_context(nc.psum_tensor("ps_ob", [128, 512], F32))
    ps_cb = bctx.enter_context(nc.psum_tensor("ps_cb", [128, 512], F32))
    b_pob, b_pcb = Buf("pob"), Buf("pcb")
    BSB = lambda nm, shp, dt: bctx.enter_context(nc.sbuf_tensor(nm, shp, dt))
    kh = BSB("kh", [128, 2 * TK], BF16)
    vga = BSB("vga", [128, 2 * 2 * 4160], BF16)
    ptl = BSB("ptl", [128, 2 * 2048], BF16)
    zr = BSB("zr", [1, 2 * 512], F32)
    zb = BSB("zb", [64, 2 * 512], F32)
    b_kh = [Buf(), Buf()]
    b_vga = [Buf(), Buf()]
    b_ptl = [Buf(), Buf()]
    b_zr, b_zb = Buf(), Buf()

    for sl in range(2):
        op("pool", lambda e, sl=sl: e.memset(
            vga[:, sl * 8320:(sl + 1) * 8320], 1.0), writes=[b_vga[sl]])

    for hp in range(8):
        sl = hp % 2
        for v in range(2):
            h = 2 * hp + v
            for r in range(4):
                op("sync", lambda e, h=h, sl=sl, r=r, v=v: e.dma_start(
                    out=kh[v * 64:(v + 1) * 64,
                           sl * TK + r * T:sl * TK + (r + 1) * T],
                    in_=cc_kg[r * D + h * 64:r * D + h * 64 + 64, :]),
                    dma=True, waits=[ev_k],
                    writes=[b_kh[sl]] if (r == 0 and v == 0) else (),
                    writes_append=[b_kh[sl]] if (r > 0 or v > 0) else ())
        for v in range(2):
            h = 2 * hp + v
            for g in range(16):
                eng = "sync" if g % 2 == 0 else "pool"
                op(eng, lambda e, h=h, sl=sl, v=v, g=g: e.dma_start(
                    out=vga[:, sl * 8320 + v * 4160 + g * 4 * 65:
                            sl * 8320 + v * 4160 + (g + 1) * 4 * 65]
                    .rearrange("p (kt z) -> p kt z", z=65)[:, :, 0:64],
                    in_=cc_vg[g * 512:(g + 1) * 512, h * 64:(h + 1) * 64]
                    .rearrange("(kt p) z -> p kt z", p=128)),
                    dma=True, waits=[ev_v], writes_append=[b_vga[sl]])
        ch = hp
        for qb in range(4):
            qcol = ch * T + qb * 512
            if qb % 2 == 0:
                OB = [(ps_o, b_po), (ps_c, b_pc)]
            else:
                OB = [(ps_ob, b_pob), (ps_cb, b_pcb)]
            b_sh = [b_a0, b_a1]

            def emit_s(kt, sl=sl, qcol=qcol, ch=ch):
                psl = kt % 2
                for v in range(2):
                    op("pe", lambda e, sl=sl, kt=kt, v=v, psl=psl,
                        qcol=qcol: e.matmul(
                            ps_a[:, psl * 1024 + v * 512:
                                 psl * 1024 + (v + 1) * 512],
                            kh[v * 64:(v + 1) * 64,
                               sl * TK + kt * 128:sl * TK + (kt + 1) * 128],
                            q_fm[v * 64:(v + 1) * 64, qcol:qcol + 512],
                            start=True, stop=True),
                        reads=[b_kh[sl], b_qfm[ch]],
                        writes=[b_sh[psl]] if v == 0 else (),
                        writes_append=[b_sh[psl]] if v > 0 else ())

            emit_s(0)
            for kt in range(64):
                psl = kt % 2
                op("act", lambda e, psl=psl: e.activation(
                    ptl[:, psl * 1024:(psl + 1) * 1024],
                    ps_a[:, psl * 1024:(psl + 1) * 1024],
                    AF.Exp, scale=0.125),
                    reads=[b_sh[psl]], writes=[b_ptl[psl]])
                if kt + 1 < 64:
                    emit_s(kt + 1)
                for v in range(2):
                    pso, b_pso = OB[v]
                    op("pe", lambda e, sl=sl, psl=psl, kt=kt, v=v,
                        pso=pso: e.matmul(
                            pso[0:65, :],
                            vga[:, sl * 8320 + v * 4160 + kt * 65:
                                sl * 8320 + v * 4160 + (kt + 1) * 65],
                            ptl[:, psl * 1024 + v * 512:
                                psl * 1024 + (v + 1) * 512],
                            start=(kt == 0), stop=(kt == 63)),
                        reads=[b_vga[sl], b_ptl[psl]],
                        writes=[b_pso] if kt == 0 else (),
                        writes_append=[b_pso] if kt > 0 else ())
            for v in range(2):
                pso, b_pso = OB[v]
                op("dve", lambda e, v=v, pso=pso: e.reciprocal(
                    zr[:, v * 512:(v + 1) * 512], pso[64:65, :]),
                    reads=[b_pso], writes=[b_zr] if v == 0 else (),
                    writes_append=[b_zr] if v == 1 else ())
            for v in range(2):
                pso, b_pso = OB[v]
                op("pe", lambda e, v=v: e.matmul(
                    ps_a[0:64, v * 1024:v * 1024 + 512], onesr32[:, 0:64],
                    zr[:, v * 512:(v + 1) * 512],
                    start=True, stop=True),
                    reads=[b_zr, b_ones], writes=[b_sh[v]])
                op("dve", lambda e, v=v: e.tensor_copy(
                    zb[:, v * 512:(v + 1) * 512],
                    ps_a[0:64, v * 1024:v * 1024 + 512]),
                    reads=[b_sh[v]],
                    writes=[b_zb] if v == 0 else (),
                    writes_append=[b_zb] if v == 1 else ())
                op("dve", lambda e, v=v, qcol=qcol, ch=ch, pso=pso:
                    e.tensor_tensor(
                        attn_fm[v * 64:(v + 1) * 64, qcol:qcol + 512],
                        pso[0:64, :], zb[:, v * 512:(v + 1) * 512], ALU.mult),
                    reads=[b_pso, b_zb], writes_append=[b_attn[ch]])

    dump("attn", attn_fm[:], b_attn, BF16)
    dump("kg", cc_kg, [b_ccko], BF16)
    dump("vg", cc_vg, [b_ccvo], BF16)
    bctx.close()
    scope_q.close()
    s.barrier()

    cdctx = ExitStack()
    CSB = lambda nm, shp, dt: cdctx.enter_context(nc.sbuf_tensor(nm, shp, dt))
    x_post = CSB("x_post", [128, NT * D], F32)
    b_xp = [Buf(f"xp{t}") for t in range(NT)]

    c1ctx = ExitStack()
    C1SB = lambda nm, shp, dt: c1ctx.enter_context(nc.sbuf_tensor(nm, shp, dt))
    wo_sb = C1SB("wo_sb", [128, 8 * D], BF16)
    bo_s = C1SB("bo_s", [1, D], F32)
    bo_bc = C1SB("bo_bc", [128, D], F32)
    xoc = C1SB("xoc", [128, 2 * D], F32)
    c1s = C1SB("c1s", [128, 512], F32)
    b_wo, b_bos, b_bobc, b_c1s = Buf(), Buf(), Buf(), Buf()
    b_xoc = [Buf(), Buf()]

    for dc in range(8):
        op("pool", lambda e, dc=dc: e.dma_start(
            out=wo_sb[:, dc * D:(dc + 1) * D],
            in_=wo_d[dc * 128:(dc + 1) * 128, :]),
            dma=True, writes=[b_wo] if dc == 0 else (),
            writes_append=[b_wo] if dc > 0 else ())
    op("sync", lambda e: e.dma_start(out=bo_s[:], in_=bo_d), dma=True,
       writes=[b_bos])
    for j in range(2):
        op("pe", lambda e, j=j: e.matmul(
            ps_c[:, 0:512], onesr32[:, 0:128], bo_s[:, j * 512:(j + 1) * 512],
            start=True, stop=True),
            reads=[b_bos, b_ones], writes=[b_pc])
        op("dve", lambda e, j=j: e.tensor_copy(
            bo_bc[:, j * 512:(j + 1) * 512], ps_c[:, 0:512]),
            reads=[b_pc],
            writes=[b_bobc] if j == 0 else (),
            writes_append=[b_bobc] if j > 0 else ())

    for tc in range(NT):
        sl = tc % 2
        op("sync", lambda e, tc=tc, sl=sl: e.dma_start(
            out=xoc[:, sl * D:(sl + 1) * D],
            in_=x_in[tc * 128:(tc + 1) * 128, :]),
            dma=True, writes=[b_xoc[sl]])
        for dh2 in range(2):
            pbank, b_pb = BANKS[(tc * 2 + dh2) % 4]
            for dc in range(8):
                op("pe", lambda e, tc=tc, dh2=dh2, dc=dc, pbank=pbank:
                    e.matmul(
                        pbank,
                        attn_fm[:, dc * T + tc * 128:dc * T + (tc + 1) * 128],
                        wo_sb[:, dc * D + dh2 * 512:dc * D + (dh2 + 1) * 512],
                        start=(dc == 0), stop=(dc == 7)),
                    reads=[b_attn[dc], b_wo],
                    writes=[b_pb] if dc == 0 else (),
                    writes_append=[b_pb] if dc > 0 else ())
            op("dve", lambda e, dh2=dh2, pbank=pbank: e.tensor_tensor(
                c1s[:], pbank, bo_bc[:, dh2 * 512:(dh2 + 1) * 512], ALU.add),
                reads=[b_pb, b_bobc], writes=[b_c1s])
            op("dve", lambda e, tc=tc, sl=sl, dh2=dh2: e.tensor_tensor(
                x_post[:, tc * D + dh2 * 512:tc * D + (dh2 + 1) * 512],
                c1s[:], xoc[:, sl * D + dh2 * 512:sl * D + (dh2 + 1) * 512],
                ALU.add),
                reads=[b_c1s, b_xoc[sl]], writes_append=[b_xp[tc]])

    dump("xpost", x_post[:], b_xp)
    c1ctx.close()
    scope_attn.close()
    s.barrier()
    pst_ctx = ExitStack()
    ps_t = pst_ctx.enter_context(nc.psum_tensor("ps_t3", [128, 1024], BF16))
    ps_t2 = pst_ctx.enter_context(nc.psum_tensor("ps_t4", [128, 1024], BF16))

    xn2 = CSB("xn2", [128, 8 * T], BF16)
    xn2_tm = CSB("xn2_tm", [128, D], BF16)
    lns2 = CSB("lns2", [128, 16], F32)
    lnsq2 = CSB("lnsq2", [128, D], F32)
    gts = CSB("gts", [128, NT * NE], F32)
    msk = CSB("msk", [128, NT * NE], F32)
    i12 = CSB("i12", [128, NT * 2], F32)
    g12 = CSB("g12", [128, NT * 2], F32)
    t8 = CSB("t8", [128, 8], F32)
    i8u = CSB("i8u", [128, 8], U32)
    r8a = CSB("r8a", [128, 8], F32)
    r8b = CSB("r8b", [128, 8], F32)
    r1a = CSB("r1a", [128, 8], F32)
    cs_s = CSB("cs_s", [NE, NT], F32)
    cnt_s = CSB("cnt_s", [NE, 1], F32)
    cnt_g = CSB("cnt_g", [NE, NE], F32)
    pm_s = CSB("pm_s", [NE, 1], F32)
    offs_s = CSB("offs_s", [NE, 1], F32)
    offsr = CSB("offsr", [1, NE], F32)
    cst = CSB("cst", [16, NE], F32)
    carrf = CSB("carrf", [1, 128], F32)
    carr_st = CSB("carr_st", [16, NE], F32)
    b_xn2 = [[Buf(f"xn2_{i}_{t}") for t in range(NT)] for i in range(8)]
    b_xn2_all = [b for row in b_xn2 for b in row]
    _b_x2t = Buf()
    b_xn2tm = [_b_x2t, _b_x2t]
    _b_lq2 = Buf()
    b_lns2 = [Buf(), Buf()]
    b_lnsq2 = [_b_lq2, _b_lq2]
    b_gts = [Buf() for _ in range(NT)]
    b_msk = [Buf() for _ in range(NT)]
    b_i12 = [Buf() for _ in range(NT)]
    b_g12 = [Buf() for _ in range(NT)]
    b_t8, b_i8u = Buf(), Buf()
    b_r8a, b_r8b, b_r1a = Buf(), Buf(), Buf()
    b_css, b_cnts, b_cntg, b_pms = Buf(), Buf(), Buf(), Buf()
    b_offs, b_offsr, b_cst, b_carrf, b_carrst = Buf(), Buf(), Buf(), Buf(), Buf()

    def emit_ln2(src_ap, b_src_list, dst_ap, b_dst, par=0):
        o = par * 8
        b_ln2P = b_lns2[par]
        b_lq2P = b_lnsq2[par]
        lnsq2v = lnsq2[:, 0:D]
        sl_s1 = lns2[:, o:o + 1]; sl_s2 = lns2[:, o + 1:o + 2]
        sl_m = lns2[:, o + 2:o + 3]
        sl_v = lns2[:, o + 3:o + 4]; sl_r = lns2[:, o + 4:o + 5]
        sl_nm = lns2[:, o + 5:o + 6]
        op("act", lambda e: e.activation(lnsq2v, src_ap, AF.Identity,
                                         accum_out=sl_s1),
            reads=b_src_list, writes=[b_lq2P], writes_append=[b_ln2P])
        op("act", lambda e: e.activation(lnsq2v, src_ap, AF.Square,
                                         accum_out=sl_s2),
            reads=b_src_list, writes=[b_lq2P], writes_append=[b_ln2P])
        op("dve", lambda e: e.tensor_scalar(sl_m, sl_s1, 1.0 / D, None,
                                            ALU.mult),
            reads=[b_ln2P], writes_append=[b_ln2P])
        op("dve", lambda e: e.tensor_scalar(sl_v, sl_s2, 1.0 / D, None,
                                            ALU.mult),
            reads=[b_ln2P], writes_append=[b_ln2P])
        op("dve", lambda e: e.scalar_tensor_tensor(
            sl_nm, sl_m, -1.0, sl_m, ALU.mult, ALU.mult),
            reads=[b_ln2P], writes_append=[b_ln2P])
        op("dve", lambda e: e.tensor_add(sl_v, sl_v, sl_nm),
            reads=[b_ln2P], writes_append=[b_ln2P])
        op("dve", lambda e: e.tensor_scalar(sl_v, sl_v, LN_EPS, None,
                                            ALU.add),
            reads=[b_ln2P], writes_append=[b_ln2P])
        op("act", lambda e: e.activation(sl_v, sl_v, AF.Ln),
            reads=[b_ln2P], writes_append=[b_ln2P])
        op("act", lambda e: e.activation(sl_r, sl_v, AF.Exp, scale=-0.5),
            reads=[b_ln2P], writes_append=[b_ln2P])
        op("dve", lambda e: e.tensor_scalar(sl_nm, sl_m, sl_r, -1.0,
                                            ALU.mult, ALU.mult),
            reads=[b_ln2P], writes_append=[b_ln2P])
        op("dve", lambda e: e.tensor_scalar(dst_ap, src_ap, sl_r, sl_nm,
                                            ALU.mult, ALU.add),
            reads=b_src_list + [b_ln2P], writes=[b_dst])

    for tc in range(NT):
        sl = tc % 2
        emit_ln2(x_post[:, tc * D:(tc + 1) * D], [b_xp[tc]],
                 xn2_tm[:, 0:D], b_xn2tm[sl], par=sl)
        pst, b_pst = ((ps_t, b_pt) if sl == 0 else (ps_t2, b_pt2))
        for dc in range(8):
            op("pe", lambda e, sl=sl, dc=dc, pst=pst: e.transpose(
                pst[:, dc * 128:(dc + 1) * 128],
                xn2_tm[:, dc * 128:(dc + 1) * 128],
                ident[:]),
                reads=[b_xn2tm[sl], b_ident],
                writes=[b_pst] if dc == 0 else (),
                writes_append=[b_pst] if dc > 0 else ())
        for dc in range(8):
            op("dve", lambda e, tc=tc, dc=dc, pst=pst: e.tensor_copy(
                xn2[:, dc * T + tc * 128:dc * T + (tc + 1) * 128],
                pst[:, dc * 128:(dc + 1) * 128]),
                reads=[b_pst], writes_append=[b_xn2[dc][tc]])

        for dc in range(8):
            op("pe", lambda e, tc=tc, dc=dc: e.matmul(
                ps_c[:, 0:NE],
                xn2[:, dc * T + tc * 128:dc * T + (tc + 1) * 128],
                wrt[:, dc * NE:(dc + 1) * NE],
                start=(dc == 0), stop=(dc == 7)),
                reads=[b_xn2[dc][tc], b_wrt],
                writes=[b_pc] if dc == 0 else (),
                writes_append=[b_pc] if dc > 0 else ())
        gt = gts[:, tc * NE:(tc + 1) * NE]
        op("dve", lambda e: e.reduce_max(r1a[:, 0:1], ps_c[:, 0:NE],
                                         axis=AX.X),
            reads=[b_pc], writes=[b_r1a])
        op("dve", lambda e: e.tensor_scalar(r1a[:, 1:2], r1a[:, 0:1], -1.0,
                                            None, ALU.mult),
            reads=[b_r1a], writes_append=[b_r1a])
        op("act", lambda e, gt=gt: e.activation(gt, ps_c[:, 0:NE], AF.Exp,
                                                bias=r1a[:, 1:2]),
            reads=[b_pc, b_r1a], writes=[b_gts[tc]])
        op("dve", lambda e, gt=gt: e.reduce_sum(r1a[:, 2:3], gt, axis=AX.X),
            reads=[b_gts[tc]], writes_append=[b_r1a])
        op("dve", lambda e: e.reciprocal(r1a[:, 3:4], r1a[:, 2:3]),
            reads=[b_r1a], writes_append=[b_r1a])
        op("dve", lambda e, gt=gt: e.tensor_scalar(gt, gt, r1a[:, 3:4], None,
                                                   ALU.mult),
            reads=[b_r1a], writes=[b_gts[tc]])
        op("dve", lambda e, gt=gt: e.max(t8[:], gt),
            reads=[b_gts[tc]], writes=[b_t8])
        op("dve", lambda e, gt=gt: e.max_index(i8u[:], t8[:], gt),
            reads=[b_gts[tc], b_t8], writes=[b_i8u])
        op("dve", lambda e, tc=tc: e.tensor_copy(i12[:, tc * 2:tc * 2 + 2],
                                                 i8u[:, 0:2]),
            reads=[b_i8u], writes=[b_i12[tc]])
        op("dve", lambda e, tc=tc: e.tensor_copy(g12[:, tc * 2:tc * 2 + 2],
                                                 t8[:, 0:2]),
            reads=[b_t8], writes=[b_g12[tc]])
        mk = msk[:, tc * NE:(tc + 1) * NE]
        op("dve", lambda e, tc=tc: e.tensor_scalar(
            r8a[:], iota8[:], i12[:, tc * 2:tc * 2 + 1], None, ALU.is_equal),
            reads=[b_iota8, b_i12[tc]], writes=[b_r8a])
        op("dve", lambda e, tc=tc: e.tensor_scalar(
            r8b[:], iota8[:], i12[:, tc * 2 + 1:tc * 2 + 2], None,
            ALU.is_equal),
            reads=[b_iota8, b_i12[tc]], writes=[b_r8b])
        op("dve", lambda e, mk=mk: e.tensor_tensor(mk, r8a[:], r8b[:],
                                                   ALU.add),
            reads=[b_r8a, b_r8b], writes=[b_msk[tc]])
        op("pe", lambda e, mk=mk, tc=tc: e.matmul(
            ps_o[0:NE, tc:tc + 1], mk, onescol[:],
            start=True, stop=True),
            reads=[b_msk[tc], b_onescol],
            writes=[b_po] if tc == 0 else (),
            writes_append=[b_po] if tc > 0 else ())
    op("dve", lambda e: e.tensor_copy(cs_s[:], ps_o[0:NE, 0:NT]),
        reads=[b_po], writes=[b_css])
    op("dve", lambda e: e.reduce_sum(cnt_s[:], cs_s[:], axis=AX.X),
        reads=[b_css], writes=[b_cnts])
    op("sync", lambda e: e.dma_start(
        out=cc_ci.rearrange("o e -> e o"), in_=cnt_s[:]),
        dma=True, reads=[b_cnts], writes=[b_ccc_i])
    ev_cc = op("pool", lambda e: e.collective_compute(
        "AllGather", ALU.bypass,
        replica_groups=[[0, 1, 2, 3, 4, 5, 6, 7]],
        ins=[cc_ci.opt()], outs=[cc_cg.opt()]),
        cc=True, reads=[b_ccc_i], writes=[b_ccc_o])
    op("sync", lambda e: e.dma_start(out=cnt_g[:], in_=cc_cg),
        dma=True, reads=[b_ccc_o], writes=[b_cntg])
    op("sync", lambda e: e.dma_start(out=pm_s[:], in_=prefm), dma=True,
       writes=[b_pms])
    op("pe", lambda e: e.matmul(ps_c[0:NE, 0:1], cnt_g[:], pm_s[:],
                                start=True, stop=True),
        reads=[b_cntg, b_pms], writes=[b_pc])
    op("dve", lambda e: e.tensor_copy(offs_s[:], ps_c[0:NE, 0:1]),
        reads=[b_pc], writes=[b_offs])
    op("pe", lambda e: e.transpose(ps_c[0:1, 0:NE], offs_s[:],
                                   ident32[0:NE, 0:NE]),
        reads=[b_offs, b_ident32], writes=[b_pc])
    op("dve", lambda e: e.tensor_copy(offsr[:], ps_c[0:1, 0:NE]),
        reads=[b_pc], writes=[b_offsr])
    op("pe", lambda e: e.transpose(ps_c[0:NT, 0:NE], cs_s[:],
                                   ident32[0:NE, 0:NE]),
        reads=[b_css, b_ident32], writes=[b_pc])
    op("dve", lambda e: e.tensor_copy(cst[:], ps_c[0:NT, 0:NE]),
        reads=[b_pc], writes=[b_cst])
    op("pe", lambda e: e.matmul(ps_c[0:NT, 0:NE], sl16[:], cst[:],
                                start=True, stop=False),
        reads=[b_sl16, b_cst], writes=[b_pc])
    op("pe", lambda e: e.matmul(ps_c[0:NT, 0:NE], onesr32[:, 0:NT], offsr[:],
                                start=False, stop=True),
        reads=[b_ones, b_offsr], writes_append=[b_pc])
    op("dve", lambda e: e.tensor_copy(carr_st[:], ps_c[0:NT, 0:NE]),
        reads=[b_pc], writes=[b_carrst])
    op("sync", lambda e: e.dma_start(out=carr_d, in_=carr_st[:]),
        dma=True, reads=[b_carrst], writes=[b_carr_d])
    op("sync", lambda e: e.dma_start(
        out=carrf[:], in_=carr_d.rearrange("t e -> (t e)").unsqueeze(0)),
        dma=True, reads=[b_carr_d], writes=[b_carrf])

    for tc in range(NT):
        mk = msk[:, tc * NE:(tc + 1) * NE]
        op("pe", lambda e, mk=mk: e.matmul(
            ps_c[:, 0:NE], tri_incl[:], mk, start=True, stop=False),
            reads=[b_tri, b_msk[tc]], writes=[b_pc])
        op("pe", lambda e, tc=tc: e.matmul(
            ps_c[:, 0:NE], onesr32[:, 0:128],
            carrf[:, tc * NE:(tc + 1) * NE], start=False, stop=True),
            reads=[b_ones, b_carrf], writes_append=[b_pc])
        op("dve", lambda e: e.tensor_scalar(
            r8a[:], ps_c[:, 0:NE], CAP + 0.5, None, ALU.is_lt),
            reads=[b_pc], writes=[b_r8a])
        op("dve", lambda e, mk=mk: e.tensor_tensor(r8a[:], r8a[:], mk,
                                                   ALU.mult),
            reads=[b_msk[tc]], writes=[b_r8a])
        op("dve", lambda e, tc=tc: e.tensor_scalar(
            r8b[:], iota8[:], i12[:, tc * 2:tc * 2 + 1], None, ALU.is_equal),
            reads=[b_iota8, b_i12[tc]], writes=[b_r8b])
        op("dve", lambda e: e.scalar_tensor_tensor(
            r8b[:], r8b[:], 1.0, r8a[:], ALU.mult, ALU.mult,
            accum_out=r1a[:, 0:1]),
            reads=[b_r8a], writes=[b_r8b], writes_append=[b_r1a])
        op("dve", lambda e, tc=tc: e.tensor_scalar(
            r8b[:], iota8[:], i12[:, tc * 2 + 1:tc * 2 + 2], None,
            ALU.is_equal),
            reads=[b_iota8, b_i12[tc]], writes=[b_r8b])
        op("dve", lambda e: e.scalar_tensor_tensor(
            r8b[:], r8b[:], 1.0, r8a[:], ALU.mult, ALU.mult,
            accum_out=r1a[:, 1:2]),
            reads=[b_r8a], writes=[b_r8b], writes_append=[b_r1a])
        op("dve", lambda e, tc=tc: e.tensor_tensor(
            r1a[:, 2:3], g12[:, tc * 2:tc * 2 + 1], r1a[:, 0:1], ALU.mult),
            reads=[b_g12[tc], b_r1a], writes_append=[b_r1a])
        op("dve", lambda e, tc=tc: e.tensor_tensor(
            r1a[:, 3:4], g12[:, tc * 2 + 1:tc * 2 + 2], r1a[:, 1:2],
            ALU.mult),
            reads=[b_g12[tc], b_r1a], writes_append=[b_r1a])
        op("dve", lambda e: e.tensor_tensor(
            r1a[:, 4:5], r1a[:, 2:3], r1a[:, 3:4], ALU.add),
            reads=[b_r1a], writes_append=[b_r1a])
        op("dve", lambda e: e.tensor_scalar(
            r1a[:, 4:5], r1a[:, 4:5], MOE_EPS, None, ALU.add),
            reads=[b_r1a], writes_append=[b_r1a])
        op("dve", lambda e: e.reciprocal(r1a[:, 5:6], r1a[:, 4:5]),
            reads=[b_r1a], writes_append=[b_r1a])
        op("dve", lambda e: e.tensor_tensor(
            r1a[:, 2:3], r1a[:, 2:3], r1a[:, 5:6], ALU.mult),
            reads=[b_r1a], writes_append=[b_r1a])
        op("dve", lambda e: e.tensor_tensor(
            r1a[:, 3:4], r1a[:, 3:4], r1a[:, 5:6], ALU.mult),
            reads=[b_r1a], writes_append=[b_r1a])
        op("dve", lambda e, tc=tc: e.tensor_scalar(
            r8a[:], iota8[:], i12[:, tc * 2:tc * 2 + 1], None, ALU.is_equal),
            reads=[b_iota8, b_i12[tc]], writes=[b_r8a])
        op("dve", lambda e: e.tensor_scalar(
            r8a[:], r8a[:], r1a[:, 2:3], None, ALU.mult),
            reads=[b_r1a], writes=[b_r8a])
        op("dve", lambda e, tc=tc: e.tensor_scalar(
            r8b[:], iota8[:], i12[:, tc * 2 + 1:tc * 2 + 2], None,
            ALU.is_equal),
            reads=[b_iota8, b_i12[tc]], writes=[b_r8b])
        op("dve", lambda e: e.tensor_scalar(
            r8b[:], r8b[:], r1a[:, 3:4], None, ALU.mult),
            reads=[b_r1a], writes=[b_r8b])
        op("dve", lambda e, tc=tc: e.tensor_tensor(
            comb[:, tc * NE:(tc + 1) * NE], r8a[:], r8b[:], ALU.add),
            reads=[b_r8a, b_r8b], writes=[b_comb[tc]])

    dump("xn2", xn2[:], b_xn2_all, BF16)
    dump("gts", gts[:], b_gts)
    dump("msk", msk[:], b_msk)
    dump("i12", i12[:], b_i12)
    dump("g12", g12[:], b_g12)
    dump("comb", comb[:], b_comb)
    dump("carrf", carrf[:], [b_carrf])
    dump("cntg", cnt_g[:], [b_cntg])
    pst_ctx.close()
    s.barrier()

    ps_d = ctx.enter_context(nc.psum_tensor("ps_d", [128, 512], F32))
    ps_e = ctx.enter_context(nc.psum_tensor("ps_e", [128, 512], F32))
    b_pd, b_pe_ = Buf("pd"), Buf("pe_")
    DBANKS = BANKS + [(ps_o[:], b_po), (ps_c[:], b_pc), (ps_d[:], b_pd),
                      (ps_e[:], b_pe_)]

    c_fm = CSB("c_fm", [NE, T], BF16)
    h_sb = CSB("h_sb", [128, 16 * 512], BF16)
    cbc = cdctx.enter_context(
        nc.sbuf_tensor("cbc", [128, NE * 512], BF16, side="right"))
    he = cdctx.enter_context(
        nc.sbuf_tensor("he", [128, 2 * 16 * 512], BF16, side="right"))
    wd_rhs = cdctx.enter_context(
        nc.sbuf_tensor("wd_rhs", [128, 3 * 1024], BF16, side="right"))
    b_cfm = Buf()
    b_h = [Buf() for _ in range(16)]
    b_cbc = [Buf() for _ in range(NE)]
    b_he = [Buf(), Buf()]
    b_wdr = [Buf(), Buf(), Buf()]

    for tc in range(NT):
        pp, b_pp = (ps_d, b_pd) if tc % 2 == 0 else (ps_e, b_pe_)
        op("pe", lambda e, tc=tc, pp=pp: e.transpose(
            pp[0:NE, 0:128], comb[:, tc * NE:(tc + 1) * NE], ident32[:]),
            reads=[b_comb[tc], b_ident32], writes=[b_pp])
        op("dve", lambda e, tc=tc, pp=pp: e.tensor_copy(
            c_fm[:, tc * 128:(tc + 1) * 128], pp[0:NE, 0:128]),
            reads=[b_pp], writes_append=[b_cfm])

    dump("cfm", c_fm[:], [b_cfm], BF16)
    dump("wdt", wdt_d, [b_wdt_d], BF16)
    ev_y = []
    for tsb in range(4):
        for hc in range(16):
            pp, b_pp = (ps_o, b_po) if hc % 2 == 0 else (ps_c, b_pc)
            for dc in range(8):
                op("pe", lambda e, tsb=tsb, hc=hc, dc=dc, pp=pp: e.matmul(
                    pp[:, 0:512],
                    wgt_sb[:, dc * DH + hc * 128:dc * DH + (hc + 1) * 128],
                    xn2[:, dc * T + tsb * 512:dc * T + (tsb + 1) * 512],
                    start=(dc == 0), stop=(dc == 7)),
                    reads=[b_wgt_sb] + [b_xn2[dc][tsb * 4 + i]
                                        for i in range(4)],
                    writes=[b_pp] if dc == 0 else (),
                    writes_append=[b_pp] if dc > 0 else ())
            op("act", lambda e, hc=hc, pp=pp: e.activation(
                h_sb[:, hc * 512:(hc + 1) * 512], pp[:, 0:512], AF.Silu),
                reads=[b_pp], writes=[b_h[hc]])
        if tsb == 0:
            dump("h0", h_sb[:], b_h, BF16)
        for ei in range(NE):
            pp, b_pp = (ps_o, b_po) if ei % 2 == 0 else (ps_c, b_pc)
            op("pe", lambda e, tsb=tsb, ei=ei, pp=pp: e.matmul(
                pp[:, 0:512], sel8[:, ei * 128:(ei + 1) * 128],
                c_fm[:, tsb * 512:(tsb + 1) * 512],
                start=True, stop=True),
                reads=[b_sel8, b_cfm], writes=[b_pp])
            op("dve", lambda e, ei=ei, pp=pp: e.tensor_copy(
                cbc[:, ei * 512:(ei + 1) * 512], pp[:, 0:512]),
                reads=[b_pp], writes=[b_cbc[ei]])
        for ei in range(NE):
            esl = ei % 2
            for hc in range(16):
                op("dve", lambda e, ei=ei, esl=esl, hc=hc: e.tensor_tensor(
                    he[:, esl * 8192 + hc * 512:esl * 8192 + (hc + 1) * 512],
                    h_sb[:, hc * 512:(hc + 1) * 512],
                    cbc[:, ei * 512:(ei + 1) * 512], ALU.mult),
                    reads=[b_h[hc], b_cbc[ei]],
                    writes=[b_he[esl]] if hc == 0 else (),
                    writes_append=[b_he[esl]] if hc > 0 else ())
            if tsb == 0 and ei == 0:
                dump("he0", he[:, 0:8192], [b_he[0]], BF16)
            for hc in range(16):
                wsl = hc % 3
                op("sync", lambda e, tsb=tsb, ei=ei, hc=hc, wsl=wsl:
                    e.dma_start(
                        out=wd_rhs[:, wsl * 1024:(wsl + 1) * 1024],
                        in_=wdt_d[(ei * 16 + hc) * 128:
                                  (ei * 16 + hc + 1) * 128, :]),
                    dma=True, reads=[b_wdt_d], writes=[b_wdr[wsl]])
                for tcl in range(4):
                    for dh2 in range(2):
                        pb, b_pb = DBANKS[tcl * 2 + dh2]
                        first = (ei == 0 and hc == 0)
                        last = (ei == NE - 1 and hc == 15)
                        op("pe", lambda e, ei=ei, esl=esl, hc=hc, tcl=tcl,
                            dh2=dh2, wsl=wsl, pb=pb, first=first, last=last:
                            e.matmul(
                                pb,
                                he[:, esl * 8192 + hc * 512 + tcl * 128:
                                   esl * 8192 + hc * 512 + (tcl + 1) * 128],
                                wd_rhs[:, wsl * 1024 + dh2 * 512:
                                       wsl * 1024 + (dh2 + 1) * 512],
                                start=first, stop=last),
                            reads=[b_he[esl], b_wdr[wsl]],
                            writes=[b_pb] if first else (),
                            writes_append=[b_pb] if not first else ())
        for tcl in range(4):
            tc = tsb * 4 + tcl
            for dh2 in range(2):
                pb, b_pb = DBANKS[tcl * 2 + dh2]
                op("dve", lambda e, tc=tc, dh2=dh2, pb=pb: e.tensor_tensor(
                    x_post[:, tc * D + dh2 * 512:tc * D + (dh2 + 1) * 512],
                    x_post[:, tc * D + dh2 * 512:tc * D + (dh2 + 1) * 512],
                    pb, ALU.add),
                    reads=[b_pb, b_xp[tc]], writes=[b_xp[tc]])
            ev = op("sync", lambda e, tc=tc: e.dma_start(
                out=y_out[tc * 128:(tc + 1) * 128, :],
                in_=x_post[:, tc * D:(tc + 1) * D]),
                dma=True, reads=[b_xp[tc]])
            ev_y.append(ev)

    op("sync", lambda e: e.nop(), waits=ev_y + [ev_k, ev_v, ev_cc])
    s.barrier()
    op("sync", lambda e: e.nop())

    section, semstack = s.emit(nc)
    with nc.Block() as block:
        @block.sync
        def _(e):
            section(e, "sync")

        @block.gpsimd
        def _(e):
            section(e, "pool")

        @block.tensor
        def _(e):
            section(e, "pe")

        @block.vector
        def _(e):
            section(e, "dve")

        @block.scalar
        def _(e):
            section(e, "act")

    semstack.close()
    cdctx.close()
    ctx.close()
    return nc


def kernel(L0_attn_x, L0_attn_Wqkv, L0_attn_Wo, L0_attn_bo,
           L0_ffn_w_router, L0_ffn_w_gate, L0_ffn_w_down):
    x = np.ascontiguousarray(np.asarray(L0_attn_x, dtype=np.float32))
    wqkv = np.ascontiguousarray(np.asarray(L0_attn_Wqkv, dtype=np.float32))
    wo = np.ascontiguousarray(np.asarray(L0_attn_Wo, dtype=np.float32))
    bo = np.ascontiguousarray(
        np.asarray(L0_attn_bo, dtype=np.float32).reshape(1, D))
    wr = np.ascontiguousarray(np.asarray(L0_ffn_w_router, dtype=np.float32))
    wg = np.ascontiguousarray(np.asarray(L0_ffn_w_gate, dtype=np.float32))
    wd = np.ascontiguousarray(
        np.asarray(L0_ffn_w_down, dtype=np.float32).reshape(NE * D, DH))

    nc = build_program()
    in_maps = []
    for c in range(8):
        bc, sc = c // 4, c % 4
        xs = np.ascontiguousarray(x[bc, sc * T:(sc + 1) * T, :])
        posv = np.full((128, 1), float(sc * T), dtype=np.float32)
        prefm = np.array([[1.0] if r < c else [0.0] for r in range(NE)],
                         dtype=np.float32)
        in_maps.append({
            "x_in": xs, "wqkv": wqkv, "wo": wo, "bo": bo,
            "wrout": wr, "wgate": wg, "wdown": wd,
            "posv": posv, "prefm": prefm,
        })
    res = run_bass_kernel_spmd(nc, in_maps, core_ids=list(range(8)))
    global LAST_RESULTS
    LAST_RESULTS = res.results
    out = np.empty((2, TK, D), dtype=np.float32)
    for c in range(8):
        bc, sc = c // 4, c % 4
        out[bc, sc * T:(sc + 1) * T, :] = res.results[c]["y_out"]
    return out
